# version 03
# baseline (speedup 1.0000x reference)
import numpy as np
import ml_dtypes
from contextlib import ExitStack

import concourse.bass as bass
import concourse.mybir as mybir
from concourse.bass_utils import run_bass_kernel_spmd

F32 = mybir.dt.float32
BF16 = mybir.dt.bfloat16
AF = mybir.ActivationFunctionType
ALU = mybir.AluOpType
AX = mybir.AxisListType

H = 1024
NH = 8
I_DIM = 768
NI = 6
KTOP = 4
EPS = 1e-5
SCALE = 1.0 / 32.0


class Plan:
    ENGINES = ("tensor", "vector", "scalar", "sync", "gpsimd")

    def __init__(self):
        self.ops = {e: [] for e in self.ENGINES}
        self.n = {}
        self.last_wait = {e: {} for e in self.ENGINES}

    def cnt(self, sem):
        return self.n.get(sem, 0)

    def op(self, eng, fn, waits=(), inc=None, inc_n=1):
        w = {}
        for s, v in waits:
            if v is None or v <= 0:
                continue
            if self.last_wait[eng].get(s, 0) >= v:
                continue
            w[s] = max(w.get(s, 0), v)
        for s, v in w.items():
            self.last_wait[eng][s] = v
        tok = None
        if inc is not None:
            self.n[inc] = self.n.get(inc, 0) + inc_n
            tok = self.n[inc]
        self.ops[eng].append((w, fn, inc, inc_n))
        return tok

    def v(self, fn, waits=()):
        return self.op("vector", fn, waits=[("V", self.cnt("V"))] + list(waits), inc="V")

    def s(self, fn, waits=()):
        return self.op("scalar", fn, waits=[("S", self.cnt("S"))] + list(waits), inc="S")

    def t(self, fn, waits=(), inc="T"):
        return self.op("tensor", fn, waits=waits, inc=inc)


def bcast_inner(ap2d, seg, rep):
    steps = [list(p) for p in ap2d.ap[:-1]]
    return bass.AP(ap2d.tensor, ap2d.offset, steps + [[1, seg], [0, rep]])


def strided_cols(handle, off, step, cnt):
    ap = handle[:, :]
    steps = [list(ap.ap[0])] + [[step, cnt]]
    return bass.AP(ap.tensor, off, steps)


def build_nc(NT, E):
    T = NT * 128
    NQ = max(1, T // 512)
    NCH = max(1, T // 512)
    NE_COL = NT * E
    NG = (NT + 3) // 4

    nc = bass.Bass("TRN2", target_bir_lowering=False)

    NXC = min(4, NT)
    xc_tiles = NT // NXC
    xp = nc.dram_tensor("xp", [128, NT * H], F32, kind="ExternalInput").ap()
    wq = nc.dram_tensor("wq", [128, NH * H], F32, kind="ExternalInput").ap()
    bqd = nc.dram_tensor("bqd", [1, H], F32, kind="ExternalInput").ap()
    gwd = nc.dram_tensor("gwd", [128, NH * E], F32, kind="ExternalInput").ap()
    ebd = nc.dram_tensor("ebd", [128, NE_COL], F32, kind="ExternalInput").ap()
    idfd = nc.dram_tensor("idfd", [128, 128], F32, kind="ExternalInput").ap()
    idbd = nc.dram_tensor("idbd", [128, 128], BF16, kind="ExternalInput").ap()
    w1d = nc.dram_tensor("w1d", [E, 128, NH * I_DIM], F32, kind="ExternalInput").ap()
    w3d = nc.dram_tensor("w3d", [E, 128, NH * I_DIM], F32, kind="ExternalInput").ap()
    w2d = nc.dram_tensor("w2d", [E, 128, NI * H], F32, kind="ExternalInput").ap()
    outd = nc.dram_tensor("out", [128, NT * H], F32, kind="ExternalOutput").ap()

    A_LN1T = 0
    A_WQ = NT * 1024
    A_SIZE = max(A_WQ + NH * H, NT * T, 32768)
    WB = NH * I_DIM
    B0_W1, B0_W3, B0_W2 = 0, WB, 2 * WB
    B1_W1, B1_W3 = 3 * WB, 4 * WB
    B1_W2A = 5 * WB
    B1_W2A_LEN = A_SIZE - B1_W2A
    assert A_SIZE >= B1_W2A
    B1_W2B_LEN = WB - B1_W2A_LEN

    B_AT = 4096
    B_S = B_AT + 2 * NI * 512
    B_SIZE = max(NT * 1024, B_S + 2 * NI * 512)
    B_MACC = B_AT
    B_MSK = B_AT + NE_COL

    ST_BN = 0
    ST_MV = 24
    ST_RSTD = 24 + 2 * NT
    ST_NEGMUR = ST_RSTD + NT
    ST_ALPHA = ST_NEGMUR + NT
    ST_Z4 = ST_ALPHA + NT
    ST_ZSUM = ST_Z4 + 2 * NQ
    ST_RMAX = ST_ZSUM + 2
    ST_DEN = ST_RMAX + NT
    ST_RCP = ST_DEN + NT
    ST_TMP = ST_RCP + NT
    ST_SIZE = ST_TMP + 12
    assert ST_SIZE <= 192

    st = ExitStack()
    with st:
        T_x = st.enter_context(nc.sbuf_tensor("T_x", [128, NT * H], F32))
        T_fm = st.enter_context(nc.sbuf_tensor("T_fm", [128, NT * 1024], BF16))
        T_A = st.enter_context(nc.sbuf_tensor("T_A", [128, A_SIZE], BF16))
        T_B = st.enter_context(nc.sbuf_tensor("T_B", [128, B_SIZE], BF16))
        T_st = st.enter_context(nc.sbuf_tensor("T_st", [128, 192], F32))
        T_c = st.enter_context(nc.sbuf_tensor("T_c", [128, NE_COL], F32))
        T_lg = st.enter_context(nc.sbuf_tensor("T_lg", [128, NE_COL], F32))
        T_eb = st.enter_context(nc.sbuf_tensor("T_eb", [128, NE_COL], F32))
        T_gw = st.enter_context(nc.sbuf_tensor("T_gw", [128, NH * E], F32))
        T_x2t = st.enter_context(nc.sbuf_tensor("T_x2t", [128, 1024], F32))
        T_srep = st.enter_context(nc.sbuf_tensor("T_srep", [128, E], F32))
        T_ones = st.enter_context(nc.sbuf_tensor("T_ones", [128, 128], F32))
        bq_bf = st.enter_context(nc.sbuf_tensor("bq_bf", [1, H], BF16))
        ones_b = st.enter_context(nc.sbuf_tensor("ones_b", [1, 128], BF16))
        id_b = st.enter_context(nc.sbuf_tensor("id_b", [128, 128], BF16))
        id_f = st.enter_context(nc.sbuf_tensor("id_f", [128, 128], F32))

        P0 = st.enter_context(nc.psum_tensor("P0", [128, 2048], F32))
        P1 = st.enter_context(nc.psum_tensor("P1", [128, 1024], F32))
        PTB = st.enter_context(nc.psum_tensor("PTB", [128, 1024], BF16))
        PTB2 = st.enter_context(nc.psum_tensor("PTB2", [128, 1024], BF16))

        sems = {}
        sem_names = ["dW", "dA", "dB", "dO", "dI", "dG", "T", "S", "V"] + \
            [f"dX{i}" for i in range(NXC)]
        for name in sem_names:
            sems[name] = st.enter_context(nc.semaphore(name))

        p = Plan()

        def d(fn, eng="sync", waits=(), inc="dX0"):
            return p.op(eng, fn, waits=waits, inc=inc, inc_n=16)

        d(lambda e: e.dma_start(out=T_x[:, 0:xc_tiles * H],
                                in_=xp[:, 0:xc_tiles * H]), inc="dX0")
        d(lambda e: e.dma_start(out=id_b[:, :], in_=idbd), inc="dI")
        d(lambda e: e.dma_start(out=id_f[:, :], in_=idfd), inc="dI")
        for ci in range(1, NXC):
            d(lambda e, ci=ci: e.dma_start(
                out=T_x[:, ci * xc_tiles * H:(ci + 1) * xc_tiles * H],
                in_=xp[:, ci * xc_tiles * H:(ci + 1) * xc_tiles * H]),
                inc=f"dX{ci}")
        d(lambda e: e.dma_start(out=T_gw[:, :], in_=gwd), inc="dG")
        d(lambda e: e.dma_start(out=T_eb[:, :], in_=ebd), inc="dG")
        dG_all = p.cnt("dG")

        d(lambda e: e.dma_start(out=T_A[:, A_WQ:A_WQ + NH * H], in_=wq),
          eng="gpsimd", inc="dW")
        d(lambda e: e.dma_start(out=bq_bf[:, :], in_=bqd), eng="gpsimd", inc="dW")
        dW_all = p.cnt("dW")

        v_ones_b = p.v(lambda e: e.memset(ones_b[:, :], 1.0))
        v_ones_m = p.v(lambda e: e.memset(T_ones[:, :], 1.0))

        def col(c0, n=1):
            return T_st[:, c0:c0 + n]

        def emit_ln_group(g, x_of, lnf_of, xwaits_of):
            tiles = list(range(4 * g, min(4 * g + 4, NT)))
            for j in tiles:
                sl = ST_BN + (j % 2) * 12
                x_ap = x_of(j)
                p.v(lambda e, a=x_ap, sl=sl: e.bn_stats(col(sl, 6), a[:, 0:512]),
                    waits=xwaits_of(j))
                p.v(lambda e, a=x_ap, sl=sl: e.bn_stats(col(sl + 6, 6), a[:, 512:1024]))
                p.v(lambda e, j=j, sl=sl: e.bn_aggr(col(ST_MV + 2 * j, 2), col(sl, 12)))
            nt_g = len(tiles)
            j0 = tiles[0]
            var_s = strided_cols(T_st, ST_MV + 2 * j0 + 1, 2, nt_g)
            mean_s = strided_cols(T_st, ST_MV + 2 * j0, 2, nt_g)
            rstd = col(ST_RSTD + j0, nt_g)
            negmur = col(ST_NEGMUR + j0, nt_g)
            t_y2 = col(ST_TMP, nt_g)
            t_y2v = col(ST_TMP + 4, nt_g)
            t_f = col(ST_TMP + 8, nt_g)
            vv = p.v(lambda e: e.tensor_scalar_add(var_s, var_s, EPS))
            ts = p.s(lambda e: e.activation(t_y2, var_s, AF.Sqrt), waits=[("V", vv)])
            p.v(lambda e: e.reciprocal(rstd, t_y2), waits=[("S", ts)])
            p.v(lambda e: e.tensor_tensor(t_y2, rstd, rstd, ALU.mult))
            p.v(lambda e: e.tensor_tensor(t_y2v, t_y2, var_s, ALU.mult))
            p.v(lambda e: e.tensor_scalar(t_f, t_y2v, -0.5, 1.5, ALU.mult, ALU.add))
            p.v(lambda e: e.tensor_tensor(rstd, rstd, t_f, ALU.mult))
            p.v(lambda e: e.scalar_tensor_tensor(negmur, mean_s, -1.0, rstd,
                                                 ALU.mult, ALU.mult))
            toks = []
            for j in tiles:
                toks.append(p.v(lambda e, j=j, o=lnf_of(j), a=x_of(j): e.tensor_scalar(
                    o, a, col(ST_RSTD + j), col(ST_NEGMUR + j), ALU.mult, ALU.add)))
            return toks

        def fmv(handle, j):
            return handle[:, 0:NT * 1024].rearrange(
                "p (c t) -> p c t", c=NH)[:, :, j * 128:(j + 1) * 128]

        s_ev_ln1 = [None] * NT
        t_q = [None] * NT
        s_ev_qtok = [None] * NT
        t_tr2 = [None] * NT
        s_ev_qt = [None] * NT
        t_tr1 = [None] * NT
        lnf_toks = [None] * NT

        for g in range(NG):
            toks = emit_ln_group(
                g,
                x_of=lambda j: T_x[:, j * H:(j + 1) * H],
                lnf_of=lambda j: T_B[:, j * 1024:(j + 1) * 1024],
                xwaits_of=lambda j: [(f"dX{j // xc_tiles}", 16)])
            for i, j in enumerate(range(4 * g, min(4 * g + 4, NT))):
                lnf_toks[j] = toks[i]
            for j in range(4 * g, min(4 * g + 4, NT)):
                def tr_ln(e, j=j):
                    for k in range(NH):
                        ins = e.transpose(
                            PTB[:, k * 128:(k + 1) * 128],
                            T_B[:, j * 1024 + k * 128: j * 1024 + k * 128 + 128],
                            id_b[:, :])
                    return ins
                t_tr1[j] = p.t(tr_ln, waits=[("V", lnf_toks[j]), ("dI", 32),
                                             ("S", s_ev_ln1[j - 1] if j else 0)])
                s_ev_ln1[j] = p.s(lambda e, j=j: e.copy(
                    fmv(T_A, j),
                    PTB[:, :].rearrange("p (c t) -> p c t", c=NH)),
                    waits=[("T", t_tr1[j])])

                def qmm(e, j=j):
                    base = (j % 2) * 1024
                    for half in range(2):
                        for hc in range(NH):
                            e.matmul(P0[:, base + half * 512: base + half * 512 + 512],
                                     T_A[:, A_LN1T + hc * (NT * 128) + j * 128:
                                         A_LN1T + hc * (NT * 128) + j * 128 + 128],
                                     T_A[:, A_WQ + hc * H + half * 512:
                                         A_WQ + hc * H + half * 512 + 512],
                                     start=(hc == 0), stop=False)
                        ins = e.matmul(P0[:, base + half * 512: base + half * 512 + 512],
                                       ones_b[:, :],
                                       bq_bf[:, half * 512: half * 512 + 512],
                                       start=False, stop=True)
                    return ins
                t_q[j] = p.t(qmm, waits=[("S", s_ev_ln1[j]), ("dW", dW_all),
                                         ("V", v_ones_b),
                                         ("S", s_ev_qtok[j - 2] if j >= 2 else 0)])
                s_ev_qtok[j] = p.s(lambda e, j=j: e.copy(
                    T_B[:, j * 1024:(j + 1) * 1024],
                    P0[:, (j % 2) * 1024:(j % 2) * 1024 + 1024]),
                    waits=[("T", t_q[j])])

                def tr_q(e, j=j):
                    for k in range(NH):
                        ins = e.transpose(
                            PTB2[:, k * 128:(k + 1) * 128],
                            T_B[:, j * 1024 + k * 128: j * 1024 + k * 128 + 128],
                            id_b[:, :])
                    return ins
                t_tr2[j] = p.t(tr_q, waits=[("S", s_ev_qtok[j]),
                                            ("S", s_ev_qt[j - 1] if j else 0)])
                s_ev_qt[j] = p.s(lambda e, j=j: e.copy(
                    fmv(T_fm, j),
                    PTB2[:, :].rearrange("p (c t) -> p c t", c=NH)),
                    waits=[("T", t_tr2[j])])

        t_sq = [[None] * NQ for _ in range(NT)]
        s_exp = [[None] * NQ for _ in range(NT)]
        v_alpha = [None] * NT
        for j in range(NT):
            for q in range(NQ):
                def smm(e, j=j, q=q):
                    for dc in range(NH):
                        ins = e.matmul(
                            P0[:, q * 512:(q + 1) * 512],
                            T_fm[:, dc * (NT * 128) + j * 128:
                                 dc * (NT * 128) + j * 128 + 128],
                            T_fm[:, dc * (NT * 128) + q * 512:
                                 dc * (NT * 128) + q * 512 + 512],
                            start=(dc == 0), stop=(dc == NH - 1))
                    return ins
                t_sq[j][q] = p.t(smm, waits=[
                    ("S", s_ev_qt[NT - 1]),
                    ("S", s_exp[j - 1][q] if j else 0)])
                s_exp[j][q] = p.s(lambda e, j=j, q=q: e.activation(
                    T_A[:, j * T + q * 512: j * T + q * 512 + 512],
                    P0[:, q * 512:(q + 1) * 512], AF.Exp, scale=SCALE),
                    waits=[("T", t_sq[j][q])])
            zsum = col(ST_ZSUM + (j % 2))
            vz = p.v(lambda e, j=j, o=zsum: e.reduce_sum(
                o, T_A[:, j * T: j * T + T], axis=AX.X),
                waits=[("S", s_exp[j][NQ - 1])])
            v_alpha[j] = p.v(lambda e, j=j, zs=zsum: e.reciprocal(
                col(ST_ALPHA + j), zs))

        t_ao = [[None] * 2 for _ in range(NT)]
        v_ev3 = [[None] * 2 for _ in range(NT)]
        s_exp_all = p.cnt("S")
        for j in range(NT):
            for hh in range(2):
                slot = (2 * j + hh) % 4
                def aomm(e, j=j, hh=hh, slot=slot):
                    for i in range(NT):
                        ins = e.matmul(
                            P0[:, slot * 512: slot * 512 + 512],
                            T_A[:, i * T + j * 128: i * T + j * 128 + 128],
                            T_B[:, i * 1024 + hh * 512: i * 1024 + hh * 512 + 512],
                            start=(i == 0), stop=(i == NT - 1))
                    return ins
                prev = v_ev3[(2 * j + hh - 4) // 2][(2 * j + hh - 4) % 2] \
                    if 2 * j + hh >= 4 else 0
                t_ao[j][hh] = p.t(aomm, waits=[("S", s_exp_all), ("V", prev)])
                v_ev3[j][hh] = p.v(lambda e, j=j, hh=hh, slot=slot: e.scalar_tensor_tensor(
                    T_x[:, j * H + hh * 512: j * H + hh * 512 + 512],
                    P0[:, slot * 512: slot * 512 + 512],
                    col(ST_ALPHA + j),
                    T_x[:, j * H + hh * 512: j * H + hh * 512 + 512],
                    ALU.mult, ALU.add),
                    waits=[("T", t_ao[j][hh]), ("V", v_alpha[j])])

        t_p3_all = p.cnt("T")
        v_p3_all = p.cnt("V")

        def srep_mm(e):
            for hc in range(NH):
                ins = e.matmul(P0[:, 512:512 + E], T_ones[:, :],
                               T_gw[:, hc * E:(hc + 1) * E],
                               start=(hc == 0), stop=(hc == NH - 1))
            return ins
        t_srep = p.t(srep_mm, waits=[("V", v_ones_m), ("dG", dG_all),
                                     ("V", v_p3_all)])
        v_srep = p.v(lambda e: e.tensor_copy(T_srep[:, :], P0[:, 512:512 + E]),
                     waits=[("T", t_srep)])

        s_ev_ln2 = [None] * NT
        s_x2t = [None] * NT
        t_x2tr = [None] * NT
        t_tr4 = [None] * NT
        t_rt = [None] * NT
        s_lgev = [None] * NT
        v_fix = [None] * NT
        lnf2_toks = [None] * NT

        def emit_fixups(tiles):
            for j in tiles:
                p.v(lambda e, j=j: e.tensor_scalar_mul(
                    T_lg[:, j * E:(j + 1) * E], T_lg[:, j * E:(j + 1) * E],
                    col(ST_RSTD + j)),
                    waits=[("S", s_lgev[j])])
                v_fix[j] = p.v(lambda e, j=j: e.scalar_tensor_tensor(
                    T_lg[:, j * E:(j + 1) * E], T_srep[:, :],
                    col(ST_NEGMUR + j), T_lg[:, j * E:(j + 1) * E],
                    ALU.mult, ALU.add))

        for g in range(NG):
            toks = emit_ln_group(
                g,
                x_of=lambda j: T_x[:, j * H:(j + 1) * H],
                lnf_of=lambda j: T_B[:, j * 1024:(j + 1) * 1024],
                xwaits_of=lambda j: [("V", v_ev3[j][1]), ("T", t_p3_all),
                                     ("V", v_srep)])
            for i, j in enumerate(range(4 * g, min(4 * g + 4, NT))):
                lnf2_toks[j] = toks[i]
            for j in range(4 * g, min(4 * g + 4, NT)):
                def x2tr(e, j=j):
                    for k in range(NH):
                        ins = e.transpose(
                            P1[:, k * 128:(k + 1) * 128],
                            T_x[:, j * H + k * 128: j * H + k * 128 + 128],
                            id_f[:, :])
                    return ins
                t_x2tr[j] = p.t(x2tr, waits=[("V", v_ev3[j][1]),
                                             ("S", s_x2t[j - 1] if j else 0)])
                s_x2t[j] = p.s(lambda e: e.copy(T_x2t[:, :], P1[:, :]),
                               waits=[("T", t_x2tr[j])])

                def tr_ln2(e, j=j):
                    for k in range(NH):
                        ins = e.transpose(
                            PTB[:, k * 128:(k + 1) * 128],
                            T_B[:, j * 1024 + k * 128: j * 1024 + k * 128 + 128],
                            id_b[:, :])
                    return ins
                t_tr4[j] = p.t(tr_ln2, waits=[("V", lnf2_toks[j]),
                                              ("S", s_ev_ln2[j - 1] if j else 0)])
                s_ev_ln2[j] = p.s(lambda e, j=j: e.copy(
                    fmv(T_fm, j),
                    PTB[:, :].rearrange("p (c t) -> p c t", c=NH)),
                    waits=[("T", t_tr4[j])])

                def rtmm(e, j=j):
                    for hc in range(NH):
                        ins = e.matmul(P0[:, 0:E],
                                       T_x2t[:, hc * 128:(hc + 1) * 128],
                                       T_gw[:, hc * E:(hc + 1) * E],
                                       start=(hc == 0), stop=(hc == NH - 1))
                    return ins
                t_rt[j] = p.t(rtmm, waits=[("S", s_x2t[j]), ("dG", dG_all),
                                           ("S", s_lgev[j - 1] if j else 0)])
                s_lgev[j] = p.s(lambda e, j=j: e.copy(
                    T_lg[:, j * E:(j + 1) * E], P0[:, 0:E]),
                    waits=[("T", t_rt[j])])
            if g > 0:
                emit_fixups(range(4 * (g - 1), 4 * g))
        emit_fixups(range(4 * (NG - 1), NT))

        macc = T_B[:, B_MACC:B_MACC + NE_COL]
        msk = T_B[:, B_MSK:B_MSK + NE_COL]
        rmax = col(ST_RMAX, NT)
        den = col(ST_DEN, NT)
        rcpd = col(ST_RCP, NT)

        s_sig = p.s(lambda e: e.activation(T_lg[:, :], T_lg[:, :], AF.Sigmoid),
                    waits=[("V", v_fix[NT - 1])])
        d3 = lambda ap: ap.rearrange("p (s e) -> p s e", s=NT)
        p.v(lambda e: e.tensor_add(T_eb[:, :], T_lg[:, :], T_eb[:, :]),
            waits=[("S", s_sig)])
        for k in range(KTOP):
            p.v(lambda e: e.reduce_max(
                rmax.rearrange("p (s o) -> p s o", s=NT), d3(T_eb[:, :]), axis=AX.X))
            p.v(lambda e: e.tensor_tensor(
                d3(msk), d3(T_eb[:, :]), bcast_inner(rmax, NT, E), ALU.is_ge))
            if k == 0:
                p.v(lambda e: e.tensor_copy(macc, msk))
            else:
                p.v(lambda e: e.tensor_add(macc, macc, msk))
            if k < KTOP - 1:
                p.v(lambda e: e.scalar_tensor_tensor(
                    T_eb[:, :], msk, -1e30, T_eb[:, :], ALU.mult, ALU.add))
        p.v(lambda e: e.tensor_tensor(T_lg[:, :], T_lg[:, :], macc, ALU.mult))
        p.v(lambda e: e.reduce_sum(
            den.rearrange("p (s o) -> p s o", s=NT), d3(T_lg[:, :]), axis=AX.X))
        p.v(lambda e: e.reciprocal(rcpd, den))
        v_c = p.v(lambda e: e.tensor_tensor(
            d3(T_c[:, :]), d3(T_lg[:, :]), bcast_inner(rcpd, NT, E), ALU.mult))

        s_ln2_all = s_lgev[NT - 1]

        def w1_ap(par, hc, ic):
            off = (B0_W1 if par == 0 else B1_W1) + hc * I_DIM + ic * 128
            return T_A[:, off: off + 128]

        def w3_ap(par, hc, ic):
            off = (B0_W3 if par == 0 else B1_W3) + hc * I_DIM + ic * 128
            return T_A[:, off: off + 128]

        def w2_ap(par, ic, hh):
            off = ic * H + hh * 512
            if par == 0:
                return T_A[:, B0_W2 + off: B0_W2 + off + 512]
            if off + 512 <= B1_W2A_LEN:
                return T_A[:, B1_W2A + off: B1_W2A + off + 512]
            off -= B1_W2A_LEN
            return T_B[:, off: off + 512]

        def at_ap(par, ic, sl=None):
            off = B_AT + par * (NI * 512) + ic * 512
            if sl is None:
                return T_B[:, off: off + 512]
            return T_B[:, off + sl[0]: off + sl[1]]

        def s_ap(par, ic):
            off = B_S + par * (NI * 512) + ic * 512
            return T_B[:, off: off + 512]

        exp_dma_tok = [None] * E
        t_out_done = [None] * E

        def issue_expert(e_idx, gate_toks):
            par = e_idx % 2
            sem = "dA" if par == 0 else "dB"
            p.op("gpsimd", lambda e, i=e_idx, par=par: e.dma_start(
                out=T_A[:, (B0_W1 if par == 0 else B1_W1):
                        (B0_W1 if par == 0 else B1_W1) + WB],
                in_=w1d[i, :, :]), waits=gate_toks, inc=sem, inc_n=16)
            p.op("gpsimd", lambda e, i=e_idx, par=par: e.dma_start(
                out=T_A[:, (B0_W3 if par == 0 else B1_W3):
                        (B0_W3 if par == 0 else B1_W3) + WB],
                in_=w3d[i, :, :]), inc=sem, inc_n=16)
            if par == 0:
                p.op("gpsimd", lambda e, i=e_idx: e.dma_start(
                    out=T_A[:, B0_W2: B0_W2 + WB], in_=w2d[i, :, :]),
                    inc=sem, inc_n=16)
            else:
                p.op("gpsimd", lambda e, i=e_idx: e.dma_start(
                    out=T_A[:, B1_W2A: B1_W2A + B1_W2A_LEN],
                    in_=w2d[i, :, 0:B1_W2A_LEN]), inc=sem, inc_n=16)
                p.op("gpsimd", lambda e, i=e_idx: e.dma_start(
                    out=T_B[:, 0:B1_W2B_LEN],
                    in_=w2d[i, :, B1_W2A_LEN:WB]), inc=sem, inc_n=16)
            exp_dma_tok[e_idx] = (sem, p.cnt(sem))

        issue_expert(0, [("T", t_p3_all)])
        if E > 1:
            issue_expert(1, [("T", t_tr4[min(3, NT - 1)])])

        NGC = E * NCH
        t_g = {}
        t_u = {}
        s_silu = {}
        v_a = {}
        t_og = {}
        v_oev = {}
        out_slots = [P1[:, 0:512], P1[:, 512:1024]]
        oslot_war = {0: ("S", s_x2t[NT - 1]), 1: ("S", s_x2t[NT - 1])}
        bank_war = {0: ("S", s_ln2_all), 1: ("V", v_srep),
                    2: ("V", v_p3_all), 3: ("V", v_p3_all)}

        def emit_chunk(gc):
            e_i, c = divmod(gc, NCH)
            par = e_i % 2
            gpar = gc % 2
            sem, dval = exp_dma_tok[e_i]
            for ic in range(NI):
                gb, ub = ic % 2, 2 + ic % 2

                def gmm(e, c=c, par=par, ic=ic, gb=gb):
                    for hc in range(NH):
                        ins = e.matmul(
                            P0[:, gb * 512: gb * 512 + 512],
                            w1_ap(par, hc, ic),
                            T_fm[:, hc * (NT * 128) + c * 512:
                                 hc * (NT * 128) + c * 512 + 512],
                            start=(hc == 0), stop=(hc == NH - 1))
                    return ins
                t_g[(gc, ic)] = p.t(gmm, waits=[(sem, dval), ("S", s_ln2_all),
                                                bank_war[gb]])

                def umm(e, c=c, par=par, ic=ic, ub=ub):
                    for hc in range(NH):
                        ins = e.matmul(
                            P0[:, ub * 512: ub * 512 + 512],
                            w3_ap(par, hc, ic),
                            T_fm[:, hc * (NT * 128) + c * 512:
                                 hc * (NT * 128) + c * 512 + 512],
                            start=(hc == 0), stop=(hc == NH - 1))
                    return ins
                t_u[(gc, ic)] = p.t(umm, waits=[bank_war[ub]])

                w = [("T", t_g[(gc, ic)])]
                if gc >= 2:
                    w.append(("V", v_a[(gc - 2, ic)]))
                s_silu[(gc, ic)] = p.s(lambda e, gpar=gpar, ic=ic, gb=gb: e.activation(
                    s_ap(gpar, ic), P0[:, gb * 512: gb * 512 + 512], AF.Silu),
                    waits=w)
                bank_war[gb] = ("S", s_silu[(gc, ic)])
                w = [("T", t_u[(gc, ic)]), ("S", s_silu[(gc, ic)])]
                if gc >= 2:
                    w.append(("T", t_og[(gc - 2, 7)]))
                if gc < 2:
                    w.append(("V", v_c))
                v_a[(gc, ic)] = p.v(lambda e, gpar=gpar, ic=ic, ub=ub: e.tensor_tensor(
                    at_ap(gpar, ic),
                    s_ap(gpar, ic),
                    P0[:, ub * 512: ub * 512 + 512], ALU.mult),
                    waits=w)
                bank_war[ub] = ("V", v_a[(gc, ic)])

        def emit_out(gc):
            e_i, c = divmod(gc, NCH)
            par = e_i % 2
            gpar = gc % 2
            sem, dval = exp_dma_tok[e_i]
            grp = 0
            for ts in range(4):
                for hh in range(2):
                    slot = grp % 2
                    def omm(e, par=par, gpar=gpar, ts=ts, hh=hh, slot=slot):
                        for ic in range(NI):
                            ins = e.matmul(
                                out_slots[slot],
                                at_ap(gpar, ic, (ts * 128, ts * 128 + 128)),
                                w2_ap(par, ic, hh),
                                start=(ic == 0), stop=(ic == NI - 1))
                        return ins
                    w = [("V", v_a[(gc, NI - 1)]), (sem, dval), oslot_war[slot]]
                    t_og[(gc, grp)] = p.t(omm, waits=w)
                    tile = c * 4 + ts
                    v_oev[(gc, grp)] = p.v(
                        lambda e, e_i=e_i, ts=ts, hh=hh, slot=slot, tile=tile:
                        e.scalar_tensor_tensor(
                            T_x[:, tile * H + hh * 512: tile * H + hh * 512 + 512],
                            out_slots[slot],
                            T_c[:, tile * E + e_i: tile * E + e_i + 1],
                            T_x[:, tile * H + hh * 512: tile * H + hh * 512 + 512],
                            ALU.mult, ALU.add),
                        waits=[("T", t_og[(gc, grp)]), ("V", v_c)])
                    oslot_war[slot] = ("V", v_oev[(gc, grp)])
                    grp += 1
            if c == NCH - 1:
                t_out_done[e_i] = p.cnt("T")

        def after_out(x):
            e_done, c_done = divmod(x, NCH)
            if c_done == NCH - 1 and e_done + 2 < E:
                issue_expert(e_done + 2, [("T", t_out_done[e_done])])

        emit_chunk(0)
        if NGC > 1:
            emit_chunk(1)
        for gc in range(2, NGC):
            emit_out(gc - 2)
            after_out(gc - 2)
            emit_chunk(gc)
        if NGC > 1:
            emit_out(NGC - 2)
            after_out(NGC - 2)
        emit_out(NGC - 1)

        v_final = p.cnt("V")

        p.op("sync", lambda e: e.dma_start(out=outd, in_=T_x[:, :]),
             waits=[("V", v_final)], inc="dO", inc_n=16)
        p.op("sync", lambda e: e.wait_ge(sems["dO"], 16))

        with nc.Block() as block:
            def runner(eng_name):
                def body(eng):
                    for w, fn, inc, inc_n in p.ops[eng_name]:
                        for s_name, v in sorted(w.items()):
                            eng.wait_ge(sems[s_name], v)
                        ins = fn(eng)
                        if inc is not None and ins is not None:
                            ins.then_inc(sems[inc], inc_n)
                return body
            block.tensor(runner("tensor"))
            block.vector(runner("vector"))
            block.scalar(runner("scalar"))
            block.sync(runner("sync"))
            block.gpsimd(runner("gpsimd"))

    return nc


_NC_CACHE = {}


def get_nc(NT, E):
    key = (NT, E)
    if key not in _NC_CACHE:
        _NC_CACHE[key] = build_nc(NT, E)
    return _NC_CACHE[key]


def make_shared(Wq, bq, gate_w, w1, w3, w2, e_bias, NT, E):
    f = np.float32
    wq_p = np.ascontiguousarray(
        np.asarray(Wq, f).reshape(NH, 128, H).transpose(1, 0, 2)).reshape(128, NH * H)
    bq_p = np.asarray(bq, f).reshape(1, H)
    gw_p = np.ascontiguousarray(
        np.asarray(gate_w, f).T.reshape(NH, 128, E).transpose(1, 0, 2)).reshape(128, NH * E)
    eb_p = np.ascontiguousarray(np.broadcast_to(
        np.tile(np.asarray(e_bias, f).reshape(1, E), (1, NT)), (128, NT * E)))
    w1_p = np.ascontiguousarray(
        np.asarray(w1, f).reshape(E, NH, 128, I_DIM).transpose(0, 2, 1, 3)).reshape(E, 128, NH * I_DIM)
    w3_p = np.ascontiguousarray(
        np.asarray(w3, f).reshape(E, NH, 128, I_DIM).transpose(0, 2, 1, 3)).reshape(E, 128, NH * I_DIM)
    w2_p = np.ascontiguousarray(
        np.asarray(w2, f).reshape(E, NI, 128, H).transpose(0, 2, 1, 3)).reshape(E, 128, NI * H)
    id_f = np.eye(128, dtype=f)
    id_b = np.eye(128, dtype=ml_dtypes.bfloat16)
    return {"wq": wq_p, "bqd": bq_p, "gwd": gw_p, "ebd": eb_p,
            "idfd": id_f, "idbd": id_b, "w1d": w1_p, "w3d": w3_p, "w2d": w2_p}


def kernel(L0_attn_inputs, L0_attn_Wq, L0_attn_bq, L0_ffn_gate_w,
           L0_ffn_w1, L0_ffn_w3, L0_ffn_w2, L0_ffn_e_bias):
    NT, E = 16, 32
    x = np.asarray(L0_attn_inputs, np.float32)
    B = x.shape[0]
    nc = get_nc(NT, E)
    shared = make_shared(L0_attn_Wq, L0_attn_bq, L0_ffn_gate_w,
                         L0_ffn_w1, L0_ffn_w3, L0_ffn_w2, L0_ffn_e_bias, NT, E)
    in_maps = []
    for b in range(B):
        xp = np.ascontiguousarray(
            x[b].reshape(NT, 128, H).transpose(1, 0, 2)).reshape(128, NT * H)
        in_maps.append({"xp": xp, **shared})
    res = run_bass_kernel_spmd(nc, in_maps, core_ids=list(range(B)))
    outs = [r["out"].reshape(128, NT, H).transpose(1, 0, 2).reshape(NT * 128, H)
            for r in res.results]
    return np.stack(outs).astype(np.float32)


# version 04
# speedup vs baseline: 1.1231x; 1.1231x over previous
import numpy as np
import ml_dtypes
from contextlib import ExitStack

import concourse.bass as bass
import concourse.mybir as mybir
from concourse.bass_utils import run_bass_kernel_spmd

F32 = mybir.dt.float32
BF16 = mybir.dt.bfloat16
AF = mybir.ActivationFunctionType
ALU = mybir.AluOpType
AX = mybir.AxisListType

H = 1024
NH = 8
I_DIM = 768
NI = 6
KTOP = 4
EPS = 1e-5
SCALE = 1.0 / 32.0


class Plan:
    ENGINES = ("tensor", "vector", "scalar", "sync", "gpsimd")

    def __init__(self):
        self.ops = {e: [] for e in self.ENGINES}
        self.n = {}
        self.last_wait = {e: {} for e in self.ENGINES}

    def cnt(self, sem):
        return self.n.get(sem, 0)

    def op(self, eng, fn, waits=(), inc=None, inc_n=1):
        w = {}
        for s, v in waits:
            if v is None or v <= 0:
                continue
            if self.last_wait[eng].get(s, 0) >= v:
                continue
            w[s] = max(w.get(s, 0), v)
        for s, v in w.items():
            self.last_wait[eng][s] = v
        tok = None
        if inc is not None:
            self.n[inc] = self.n.get(inc, 0) + inc_n
            tok = self.n[inc]
        self.ops[eng].append((w, fn, inc, inc_n))
        return tok

    def v(self, fn, waits=()):
        return self.op("vector", fn, waits=[("V", self.cnt("V"))] + list(waits), inc="V")

    def s(self, fn, waits=()):
        return self.op("scalar", fn, waits=[("S", self.cnt("S"))] + list(waits), inc="S")

    def t(self, fn, waits=(), inc="T"):
        return self.op("tensor", fn, waits=waits, inc=inc)


def bcast_inner(ap2d, seg, rep):
    steps = [list(p) for p in ap2d.ap[:-1]]
    return bass.AP(ap2d.tensor, ap2d.offset, steps + [[1, seg], [0, rep]])


def strided_cols(handle, off, step, cnt):
    ap = handle[:, :]
    steps = [list(ap.ap[0])] + [[step, cnt]]
    return bass.AP(ap.tensor, off, steps)


def build_nc(NT, E):
    T = NT * 128
    NQ = max(1, T // 512)
    NCH = max(1, T // 512)
    NE_COL = NT * E
    NG = (NT + 3) // 4

    nc = bass.Bass("TRN2", target_bir_lowering=False)

    NXC = min(4, NT)
    xc_tiles = NT // NXC
    xp = nc.dram_tensor("xp", [128, NT * H], F32, kind="ExternalInput").ap()
    wq = nc.dram_tensor("wq", [128, NH * H], F32, kind="ExternalInput").ap()
    bqd = nc.dram_tensor("bqd", [1, H], F32, kind="ExternalInput").ap()
    gwd = nc.dram_tensor("gwd", [128, NH * E], F32, kind="ExternalInput").ap()
    ebd = nc.dram_tensor("ebd", [128, NE_COL], F32, kind="ExternalInput").ap()
    idfd = nc.dram_tensor("idfd", [128, 128], F32, kind="ExternalInput").ap()
    idbd = nc.dram_tensor("idbd", [128, 128], BF16, kind="ExternalInput").ap()
    w1d = nc.dram_tensor("w1d", [E, 128, NH * I_DIM], F32, kind="ExternalInput").ap()
    w3d = nc.dram_tensor("w3d", [E, 128, NH * I_DIM], F32, kind="ExternalInput").ap()
    w2d = nc.dram_tensor("w2d", [E, 128, NI * H], F32, kind="ExternalInput").ap()
    outd = nc.dram_tensor("out", [128, NT * H], F32, kind="ExternalOutput").ap()

    A_LN1T = 0
    A_WQ = NT * 1024
    A_SIZE = max(A_WQ + NH * H, NT * T, 32768)
    WB = NH * I_DIM
    B0_W1, B0_W3, B0_W2 = 0, WB, 2 * WB
    B1_W1, B1_W3 = 3 * WB, 4 * WB
    B1_W2A = 5 * WB
    B1_W2A_LEN = A_SIZE - B1_W2A
    assert A_SIZE >= B1_W2A
    B1_W2B_LEN = WB - B1_W2A_LEN

    B_AT = 4096
    B_S = B_AT + 2 * NI * 512
    B_SIZE = max(NT * 1024, B_S + 2 * NI * 512)
    B_MACC = B_AT
    B_MSK = B_AT + NE_COL

    ST_BN = 0
    ST_MV = 24
    ST_RSTD = 24 + 2 * NT
    ST_NEGMUR = ST_RSTD + NT
    ST_ALPHA = ST_NEGMUR + NT
    ST_Z4 = ST_ALPHA + NT
    ST_ZSUM = ST_Z4 + 2 * NQ
    ST_RMAX = ST_ZSUM + 2
    ST_DEN = ST_RMAX + NT
    ST_RCP = ST_DEN + NT
    ST_TMP = ST_RCP + NT
    ST_SIZE = ST_TMP + 12
    assert ST_SIZE <= 192

    st = ExitStack()
    with st:
        T_x = st.enter_context(nc.sbuf_tensor("T_x", [128, NT * H], F32))
        T_fm = st.enter_context(nc.sbuf_tensor("T_fm", [128, NT * 1024], BF16))
        T_A = st.enter_context(nc.sbuf_tensor("T_A", [128, A_SIZE], BF16))
        T_B = st.enter_context(nc.sbuf_tensor("T_B", [128, B_SIZE], BF16))
        T_st = st.enter_context(nc.sbuf_tensor("T_st", [128, 192], F32))
        T_c = st.enter_context(nc.sbuf_tensor("T_c", [128, NE_COL], F32))
        T_lg = st.enter_context(nc.sbuf_tensor("T_lg", [128, NE_COL], F32))
        T_eb = st.enter_context(nc.sbuf_tensor("T_eb", [128, NE_COL], F32))
        T_gw = st.enter_context(nc.sbuf_tensor("T_gw", [128, NH * E], F32))
        T_x2t = st.enter_context(nc.sbuf_tensor("T_x2t", [128, 1024], F32))
        T_srep = st.enter_context(nc.sbuf_tensor("T_srep", [128, E], F32))
        T_ones = st.enter_context(nc.sbuf_tensor("T_ones", [128, 128], F32))
        bq_bf = st.enter_context(nc.sbuf_tensor("bq_bf", [1, H], BF16))
        ones_b = st.enter_context(nc.sbuf_tensor("ones_b", [1, 128], BF16))
        id_b = st.enter_context(nc.sbuf_tensor("id_b", [128, 128], BF16))
        id_f = st.enter_context(nc.sbuf_tensor("id_f", [128, 128], F32))

        P0 = st.enter_context(nc.psum_tensor("P0", [128, 2048], F32))
        P1 = st.enter_context(nc.psum_tensor("P1", [128, 1024], F32))
        PTB = st.enter_context(nc.psum_tensor("PTB", [128, 1024], BF16))
        PTB2 = st.enter_context(nc.psum_tensor("PTB2", [128, 1024], BF16))

        sems = {}
        sem_names = ["dW", "dA", "dB", "dO", "dI", "dG", "T", "S", "V"] + \
            [f"dX{i}" for i in range(NXC)]
        for name in sem_names:
            sems[name] = st.enter_context(nc.semaphore(name))

        p = Plan()

        def d(fn, eng="sync", waits=(), inc="dX0"):
            return p.op(eng, fn, waits=waits, inc=inc, inc_n=16)

        d(lambda e: e.dma_start(out=T_x[:, 0:xc_tiles * H],
                                in_=xp[:, 0:xc_tiles * H]), inc="dX0")
        d(lambda e: e.dma_start(out=id_b[:, :], in_=idbd), inc="dI")
        d(lambda e: e.dma_start(out=id_f[:, :], in_=idfd), inc="dI")
        for ci in range(1, NXC):
            d(lambda e, ci=ci: e.dma_start(
                out=T_x[:, ci * xc_tiles * H:(ci + 1) * xc_tiles * H],
                in_=xp[:, ci * xc_tiles * H:(ci + 1) * xc_tiles * H]),
                inc=f"dX{ci}")
        d(lambda e: e.dma_start(out=T_gw[:, :], in_=gwd), inc="dG")
        d(lambda e: e.dma_start(out=T_eb[:, :], in_=ebd), inc="dG")
        dG_all = p.cnt("dG")

        d(lambda e: e.dma_start(out=T_A[:, A_WQ:A_WQ + NH * H], in_=wq),
          eng="gpsimd", inc="dW")
        d(lambda e: e.dma_start(out=bq_bf[:, :], in_=bqd), eng="gpsimd", inc="dW")
        dW_all = p.cnt("dW")

        v_ones_b = p.v(lambda e: e.memset(ones_b[:, :], 1.0))
        v_ones_m = p.v(lambda e: e.memset(T_ones[:, :], 1.0))

        def col(c0, n=1):
            return T_st[:, c0:c0 + n]

        def emit_ln_group(g, x_of, lnf_of, xwaits_of):
            tiles = list(range(4 * g, min(4 * g + 4, NT)))
            for j in tiles:
                sl = ST_BN + (j % 2) * 12
                x_ap = x_of(j)
                p.v(lambda e, a=x_ap, sl=sl: e.bn_stats(col(sl, 6), a[:, 0:512]),
                    waits=xwaits_of(j))
                p.v(lambda e, a=x_ap, sl=sl: e.bn_stats(col(sl + 6, 6), a[:, 512:1024]))
                p.v(lambda e, j=j, sl=sl: e.bn_aggr(col(ST_MV + 2 * j, 2), col(sl, 12)))
            nt_g = len(tiles)
            j0 = tiles[0]
            var_s = strided_cols(T_st, ST_MV + 2 * j0 + 1, 2, nt_g)
            mean_s = strided_cols(T_st, ST_MV + 2 * j0, 2, nt_g)
            rstd = col(ST_RSTD + j0, nt_g)
            negmur = col(ST_NEGMUR + j0, nt_g)
            t_y2 = col(ST_TMP, nt_g)
            t_y2v = col(ST_TMP + 4, nt_g)
            t_f = col(ST_TMP + 8, nt_g)
            vv = p.v(lambda e: e.tensor_scalar_add(var_s, var_s, EPS))
            ts = p.s(lambda e: e.activation(t_y2, var_s, AF.Sqrt), waits=[("V", vv)])
            p.v(lambda e: e.reciprocal(rstd, t_y2), waits=[("S", ts)])
            p.v(lambda e: e.tensor_tensor(t_y2, rstd, rstd, ALU.mult))
            p.v(lambda e: e.tensor_tensor(t_y2v, t_y2, var_s, ALU.mult))
            p.v(lambda e: e.tensor_scalar(t_f, t_y2v, -0.5, 1.5, ALU.mult, ALU.add))
            p.v(lambda e: e.tensor_tensor(rstd, rstd, t_f, ALU.mult))
            p.v(lambda e: e.scalar_tensor_tensor(negmur, mean_s, -1.0, rstd,
                                                 ALU.mult, ALU.mult))
            toks = []
            for j in tiles:
                toks.append(p.v(lambda e, j=j, o=lnf_of(j), a=x_of(j): e.tensor_scalar(
                    o, a, col(ST_RSTD + j), col(ST_NEGMUR + j), ALU.mult, ALU.add)))
            return toks

        def fmv(handle, j):
            return handle[:, 0:NT * 1024].rearrange(
                "p (c t) -> p c t", c=NH)[:, :, j * 128:(j + 1) * 128]

        s_ev_ln1 = [None] * NT
        t_q = [None] * NT
        s_ev_qtok = [None] * NT
        t_tr2 = [None] * NT
        s_ev_qt = [None] * NT
        t_tr1 = [None] * NT
        lnf_toks = [None] * NT

        for g in range(NG):
            toks = emit_ln_group(
                g,
                x_of=lambda j: T_x[:, j * H:(j + 1) * H],
                lnf_of=lambda j: T_B[:, j * 1024:(j + 1) * 1024],
                xwaits_of=lambda j: [(f"dX{j // xc_tiles}", 16)])
            for i, j in enumerate(range(4 * g, min(4 * g + 4, NT))):
                lnf_toks[j] = toks[i]
            for j in range(4 * g, min(4 * g + 4, NT)):
                def tr_ln(e, j=j):
                    for k in range(NH):
                        ins = e.transpose(
                            PTB[:, k * 128:(k + 1) * 128],
                            T_B[:, j * 1024 + k * 128: j * 1024 + k * 128 + 128],
                            id_b[:, :])
                    return ins
                t_tr1[j] = p.t(tr_ln, waits=[("V", lnf_toks[j]), ("dI", 32),
                                             ("S", s_ev_ln1[j - 1] if j else 0)])
                s_ev_ln1[j] = p.s(lambda e, j=j: e.copy(
                    fmv(T_A, j),
                    PTB[:, :].rearrange("p (c t) -> p c t", c=NH)),
                    waits=[("T", t_tr1[j])])

                def qmm(e, j=j):
                    base = (j % 2) * 1024
                    for half in range(2):
                        for hc in range(NH):
                            e.matmul(P0[:, base + half * 512: base + half * 512 + 512],
                                     T_A[:, A_LN1T + hc * (NT * 128) + j * 128:
                                         A_LN1T + hc * (NT * 128) + j * 128 + 128],
                                     T_A[:, A_WQ + hc * H + half * 512:
                                         A_WQ + hc * H + half * 512 + 512],
                                     start=(hc == 0), stop=False)
                        ins = e.matmul(P0[:, base + half * 512: base + half * 512 + 512],
                                       ones_b[:, :],
                                       bq_bf[:, half * 512: half * 512 + 512],
                                       start=False, stop=True)
                    return ins
                t_q[j] = p.t(qmm, waits=[("S", s_ev_ln1[j]), ("dW", dW_all),
                                         ("V", v_ones_b),
                                         ("S", s_ev_qtok[j - 2] if j >= 2 else 0)])
                s_ev_qtok[j] = p.s(lambda e, j=j: e.copy(
                    T_B[:, j * 1024:(j + 1) * 1024],
                    P0[:, (j % 2) * 1024:(j % 2) * 1024 + 1024]),
                    waits=[("T", t_q[j])])

                def tr_q(e, j=j):
                    for k in range(NH):
                        ins = e.transpose(
                            PTB2[:, k * 128:(k + 1) * 128],
                            T_B[:, j * 1024 + k * 128: j * 1024 + k * 128 + 128],
                            id_b[:, :])
                    return ins
                t_tr2[j] = p.t(tr_q, waits=[("S", s_ev_qtok[j]),
                                            ("S", s_ev_qt[j - 1] if j else 0)])
                s_ev_qt[j] = p.s(lambda e, j=j: e.copy(
                    fmv(T_fm, j),
                    PTB2[:, :].rearrange("p (c t) -> p c t", c=NH)),
                    waits=[("T", t_tr2[j])])

        t_sq = [[None] * NQ for _ in range(NT)]
        s_exp = [[None] * NQ for _ in range(NT)]
        v_alpha = [None] * NT
        for j in range(NT):
            for q in range(NQ):
                def smm(e, j=j, q=q):
                    for dc in range(NH):
                        ins = e.matmul(
                            P0[:, q * 512:(q + 1) * 512],
                            T_fm[:, dc * (NT * 128) + j * 128:
                                 dc * (NT * 128) + j * 128 + 128],
                            T_fm[:, dc * (NT * 128) + q * 512:
                                 dc * (NT * 128) + q * 512 + 512],
                            start=(dc == 0), stop=(dc == NH - 1))
                    return ins
                t_sq[j][q] = p.t(smm, waits=[
                    ("S", s_ev_qt[NT - 1]),
                    ("S", s_exp[j - 1][q] if j else 0)])
                s_exp[j][q] = p.s(lambda e, j=j, q=q: e.activation(
                    T_A[:, j * T + q * 512: j * T + q * 512 + 512],
                    P0[:, q * 512:(q + 1) * 512], AF.Exp, scale=SCALE),
                    waits=[("T", t_sq[j][q])])
            zsum = col(ST_ZSUM + (j % 2))
            vz = p.v(lambda e, j=j, o=zsum: e.reduce_sum(
                o, T_A[:, j * T: j * T + T], axis=AX.X),
                waits=[("S", s_exp[j][NQ - 1])])
            v_alpha[j] = p.v(lambda e, j=j, zs=zsum: e.reciprocal(
                col(ST_ALPHA + j), zs))

        t_ao = [[None] * 2 for _ in range(NT)]
        v_ev3 = [[None] * 2 for _ in range(NT)]
        s_exp_all = p.cnt("S")
        for j in range(NT):
            for hh in range(2):
                slot = (2 * j + hh) % 4
                def aomm(e, j=j, hh=hh, slot=slot):
                    for i in range(NT):
                        ins = e.matmul(
                            P0[:, slot * 512: slot * 512 + 512],
                            T_A[:, i * T + j * 128: i * T + j * 128 + 128],
                            T_B[:, i * 1024 + hh * 512: i * 1024 + hh * 512 + 512],
                            start=(i == 0), stop=(i == NT - 1))
                    return ins
                prev = v_ev3[(2 * j + hh - 4) // 2][(2 * j + hh - 4) % 2] \
                    if 2 * j + hh >= 4 else 0
                t_ao[j][hh] = p.t(aomm, waits=[("S", s_exp_all), ("V", prev)])
                v_ev3[j][hh] = p.v(lambda e, j=j, hh=hh, slot=slot: e.scalar_tensor_tensor(
                    T_x[:, j * H + hh * 512: j * H + hh * 512 + 512],
                    P0[:, slot * 512: slot * 512 + 512],
                    col(ST_ALPHA + j),
                    T_x[:, j * H + hh * 512: j * H + hh * 512 + 512],
                    ALU.mult, ALU.add),
                    waits=[("T", t_ao[j][hh]), ("V", v_alpha[j])])

        t_p3_all = p.cnt("T")
        v_p3_all = p.cnt("V")

        def srep_mm(e):
            for hc in range(NH):
                ins = e.matmul(P0[:, 512:512 + E], T_ones[:, :],
                               T_gw[:, hc * E:(hc + 1) * E],
                               start=(hc == 0), stop=(hc == NH - 1))
            return ins
        t_srep = p.t(srep_mm, waits=[("V", v_ones_m), ("dG", dG_all),
                                     ("V", v_p3_all)])
        v_srep = p.v(lambda e: e.tensor_copy(T_srep[:, :], P0[:, 512:512 + E]),
                     waits=[("T", t_srep)])

        s_ev_ln2 = [None] * NT
        s_x2t = [None] * NT
        t_x2tr = [None] * NT
        t_tr4 = [None] * NT
        t_rt = [None] * NT
        s_lgev = [None] * NT
        v_fix = [None] * NT
        lnf2_toks = [None] * NT

        def emit_fixups(tiles):
            for j in tiles:
                p.v(lambda e, j=j: e.tensor_scalar_mul(
                    T_lg[:, j * E:(j + 1) * E], T_lg[:, j * E:(j + 1) * E],
                    col(ST_RSTD + j)),
                    waits=[("S", s_lgev[j])])
                v_fix[j] = p.v(lambda e, j=j: e.scalar_tensor_tensor(
                    T_lg[:, j * E:(j + 1) * E], T_srep[:, :],
                    col(ST_NEGMUR + j), T_lg[:, j * E:(j + 1) * E],
                    ALU.mult, ALU.add))

        for g in range(NG):
            toks = emit_ln_group(
                g,
                x_of=lambda j: T_x[:, j * H:(j + 1) * H],
                lnf_of=lambda j: T_B[:, j * 1024:(j + 1) * 1024],
                xwaits_of=lambda j: [("V", v_ev3[j][1]), ("T", t_p3_all),
                                     ("V", v_srep)])
            for i, j in enumerate(range(4 * g, min(4 * g + 4, NT))):
                lnf2_toks[j] = toks[i]
            for j in range(4 * g, min(4 * g + 4, NT)):
                def x2tr(e, j=j):
                    for k in range(NH):
                        ins = e.transpose(
                            P1[:, k * 128:(k + 1) * 128],
                            T_x[:, j * H + k * 128: j * H + k * 128 + 128],
                            id_f[:, :])
                    return ins
                t_x2tr[j] = p.t(x2tr, waits=[("V", v_ev3[j][1]),
                                             ("S", s_x2t[j - 1] if j else 0)])
                s_x2t[j] = p.s(lambda e: e.copy(T_x2t[:, :], P1[:, :]),
                               waits=[("T", t_x2tr[j])])

                def tr_ln2(e, j=j):
                    for k in range(NH):
                        ins = e.transpose(
                            PTB[:, k * 128:(k + 1) * 128],
                            T_B[:, j * 1024 + k * 128: j * 1024 + k * 128 + 128],
                            id_b[:, :])
                    return ins
                t_tr4[j] = p.t(tr_ln2, waits=[("V", lnf2_toks[j]),
                                              ("S", s_ev_ln2[j - 1] if j else 0)])
                s_ev_ln2[j] = p.s(lambda e, j=j: e.copy(
                    fmv(T_fm, j),
                    PTB[:, :].rearrange("p (c t) -> p c t", c=NH)),
                    waits=[("T", t_tr4[j])])

                def rtmm(e, j=j):
                    for hc in range(NH):
                        ins = e.matmul(P0[:, 0:E],
                                       T_x2t[:, hc * 128:(hc + 1) * 128],
                                       T_gw[:, hc * E:(hc + 1) * E],
                                       start=(hc == 0), stop=(hc == NH - 1))
                    return ins
                t_rt[j] = p.t(rtmm, waits=[("S", s_x2t[j]), ("dG", dG_all),
                                           ("S", s_lgev[j - 1] if j else 0)])
                s_lgev[j] = p.s(lambda e, j=j: e.copy(
                    T_lg[:, j * E:(j + 1) * E], P0[:, 0:E]),
                    waits=[("T", t_rt[j])])
            if g > 0:
                emit_fixups(range(4 * (g - 1), 4 * g))
        emit_fixups(range(4 * (NG - 1), NT))

        macc = T_B[:, B_MACC:B_MACC + NE_COL]
        msk = T_B[:, B_MSK:B_MSK + NE_COL]
        rmax = col(ST_RMAX, NT)
        den = col(ST_DEN, NT)
        rcpd = col(ST_RCP, NT)

        s_sig = p.s(lambda e: e.activation(T_lg[:, :], T_lg[:, :], AF.Sigmoid),
                    waits=[("V", v_fix[NT - 1])])
        d3 = lambda ap: ap.rearrange("p (s e) -> p s e", s=NT)
        p.v(lambda e: e.tensor_add(T_eb[:, :], T_lg[:, :], T_eb[:, :]),
            waits=[("S", s_sig)])
        for k in range(KTOP):
            p.v(lambda e: e.reduce_max(
                rmax.rearrange("p (s o) -> p s o", s=NT), d3(T_eb[:, :]), axis=AX.X))
            p.v(lambda e: e.tensor_tensor(
                d3(msk), d3(T_eb[:, :]), bcast_inner(rmax, NT, E), ALU.is_ge))
            if k == 0:
                p.v(lambda e: e.tensor_copy(macc, msk))
            else:
                p.v(lambda e: e.tensor_add(macc, macc, msk))
            if k < KTOP - 1:
                p.v(lambda e: e.scalar_tensor_tensor(
                    T_eb[:, :], msk, -1e30, T_eb[:, :], ALU.mult, ALU.add))
        p.v(lambda e: e.tensor_tensor(T_lg[:, :], T_lg[:, :], macc, ALU.mult))
        p.v(lambda e: e.reduce_sum(
            den.rearrange("p (s o) -> p s o", s=NT), d3(T_lg[:, :]), axis=AX.X))
        p.v(lambda e: e.reciprocal(rcpd, den))
        v_c = p.v(lambda e: e.tensor_tensor(
            d3(T_c[:, :]), d3(T_lg[:, :]), bcast_inner(rcpd, NT, E), ALU.mult))

        s_ln2_all = s_lgev[NT - 1]

        def w1_ap(par, hc, ic):
            off = (B0_W1 if par == 0 else B1_W1) + hc * I_DIM + ic * 128
            return T_A[:, off: off + 128]

        def w3_ap(par, hc, ic):
            off = (B0_W3 if par == 0 else B1_W3) + hc * I_DIM + ic * 128
            return T_A[:, off: off + 128]

        def w2_ap(par, ic, hh):
            off = ic * H + hh * 512
            if par == 0:
                return T_A[:, B0_W2 + off: B0_W2 + off + 512]
            if off + 512 <= B1_W2A_LEN:
                return T_A[:, B1_W2A + off: B1_W2A + off + 512]
            off -= B1_W2A_LEN
            return T_B[:, off: off + 512]

        def at_ap(par, ic, sl=None):
            off = B_AT + par * (NI * 512) + ic * 512
            if sl is None:
                return T_B[:, off: off + 512]
            return T_B[:, off + sl[0]: off + sl[1]]

        def s_ap(par, ic):
            off = B_S + par * (NI * 512) + ic * 512
            return T_B[:, off: off + 512]

        exp_dma_tok = [None] * E
        t_out_done = [None] * E

        def issue_expert(e_idx, gate_toks):
            par = e_idx % 2
            sem = "dA" if par == 0 else "dB"
            p.op("gpsimd", lambda e, i=e_idx, par=par: e.dma_start(
                out=T_A[:, (B0_W1 if par == 0 else B1_W1):
                        (B0_W1 if par == 0 else B1_W1) + WB],
                in_=w1d[i, :, :]), waits=gate_toks, inc=sem, inc_n=16)
            p.op("gpsimd", lambda e, i=e_idx, par=par: e.dma_start(
                out=T_A[:, (B0_W3 if par == 0 else B1_W3):
                        (B0_W3 if par == 0 else B1_W3) + WB],
                in_=w3d[i, :, :]), inc=sem, inc_n=16)
            if par == 0:
                p.op("gpsimd", lambda e, i=e_idx: e.dma_start(
                    out=T_A[:, B0_W2: B0_W2 + WB], in_=w2d[i, :, :]),
                    inc=sem, inc_n=16)
            else:
                p.op("gpsimd", lambda e, i=e_idx: e.dma_start(
                    out=T_A[:, B1_W2A: B1_W2A + B1_W2A_LEN],
                    in_=w2d[i, :, 0:B1_W2A_LEN]), inc=sem, inc_n=16)
                p.op("gpsimd", lambda e, i=e_idx: e.dma_start(
                    out=T_B[:, 0:B1_W2B_LEN],
                    in_=w2d[i, :, B1_W2A_LEN:WB]), inc=sem, inc_n=16)
            exp_dma_tok[e_idx] = (sem, p.cnt(sem))

        issue_expert(0, [("T", t_p3_all)])
        if E > 1:
            issue_expert(1, [("T", t_tr4[min(3, NT - 1)])])

        NGC = E * NCH
        t_g = {}
        t_u = {}
        s_silu = {}
        v_a = {}
        t_og = {}
        v_oev = {}
        out_slots = [P1[:, 0:512], P1[:, 512:1024]]
        oslot_war = {0: ("S", s_x2t[NT - 1]), 1: ("S", s_x2t[NT - 1])}
        bank_war = {0: ("S", s_ln2_all), 1: ("V", v_srep),
                    2: ("V", v_p3_all), 3: ("V", v_p3_all)}

        def emit_chunk(gc):
            e_i, c = divmod(gc, NCH)
            par = e_i % 2
            gpar = gc % 2
            sem, dval = exp_dma_tok[e_i]
            for ic in range(NI):
                gb, ub = ic % 2, 2 + ic % 2

                def gmm(e, c=c, par=par, ic=ic, gb=gb):
                    for hc in range(NH):
                        ins = e.matmul(
                            P0[:, gb * 512: gb * 512 + 512],
                            w1_ap(par, hc, ic),
                            T_fm[:, hc * (NT * 128) + c * 512:
                                 hc * (NT * 128) + c * 512 + 512],
                            start=(hc == 0), stop=(hc == NH - 1))
                    return ins
                t_g[(gc, ic)] = p.t(gmm, waits=[(sem, dval), ("S", s_ln2_all),
                                                bank_war[gb]])

                def umm(e, c=c, par=par, ic=ic, ub=ub):
                    for hc in range(NH):
                        ins = e.matmul(
                            P0[:, ub * 512: ub * 512 + 512],
                            w3_ap(par, hc, ic),
                            T_fm[:, hc * (NT * 128) + c * 512:
                                 hc * (NT * 128) + c * 512 + 512],
                            start=(hc == 0), stop=(hc == NH - 1))
                    return ins
                t_u[(gc, ic)] = p.t(umm, waits=[bank_war[ub]])

                w = [("T", t_g[(gc, ic)])]
                if gc >= 2:
                    w.append(("V", v_a[(gc - 2, ic)]))
                s_silu[(gc, ic)] = p.s(lambda e, gpar=gpar, ic=ic, gb=gb: e.activation(
                    s_ap(gpar, ic), P0[:, gb * 512: gb * 512 + 512], AF.Silu),
                    waits=w)
                bank_war[gb] = ("S", s_silu[(gc, ic)])
                w = [("T", t_u[(gc, ic)]), ("S", s_silu[(gc, ic)])]
                if gc >= 2:
                    w.append(("T", t_og[(gc - 2, 7)]))
                if gc < 2:
                    w.append(("V", v_c))
                v_a[(gc, ic)] = p.v(lambda e, gpar=gpar, ic=ic, ub=ub: e.tensor_tensor(
                    at_ap(gpar, ic),
                    s_ap(gpar, ic),
                    P0[:, ub * 512: ub * 512 + 512], ALU.mult),
                    waits=w)
                bank_war[ub] = ("V", v_a[(gc, ic)])

        def emit_out(gc):
            e_i, c = divmod(gc, NCH)
            par = e_i % 2
            gpar = gc % 2
            sem, dval = exp_dma_tok[e_i]
            grp = 0
            for ts in range(4):
                for hh in range(2):
                    slot = grp % 2
                    def omm(e, par=par, gpar=gpar, ts=ts, hh=hh, slot=slot):
                        for ic in range(NI):
                            ins = e.matmul(
                                out_slots[slot],
                                at_ap(gpar, ic, (ts * 128, ts * 128 + 128)),
                                w2_ap(par, ic, hh),
                                start=(ic == 0), stop=(ic == NI - 1))
                        return ins
                    w = [("V", v_a[(gc, NI - 1)]), (sem, dval), oslot_war[slot]]
                    t_og[(gc, grp)] = p.t(omm, waits=w)
                    tile = c * 4 + ts
                    v_oev[(gc, grp)] = p.v(
                        lambda e, e_i=e_i, ts=ts, hh=hh, slot=slot, tile=tile:
                        e.scalar_tensor_tensor(
                            T_x[:, tile * H + hh * 512: tile * H + hh * 512 + 512],
                            out_slots[slot],
                            T_c[:, tile * E + e_i: tile * E + e_i + 1],
                            T_x[:, tile * H + hh * 512: tile * H + hh * 512 + 512],
                            ALU.mult, ALU.add),
                        waits=[("T", t_og[(gc, grp)]), ("V", v_c)])
                    oslot_war[slot] = ("V", v_oev[(gc, grp)])
                    grp += 1
            if c == NCH - 1:
                t_out_done[e_i] = p.cnt("T")

        def after_out(x):
            e_done, c_done = divmod(x, NCH)
            if c_done == NCH - 1 and e_done + 2 < E:
                issue_expert(e_done + 2, [("T", t_out_done[e_done])])

        emit_chunk(0)
        if NGC > 1:
            emit_chunk(1)
        for gc in range(2, NGC):
            emit_out(gc - 2)
            after_out(gc - 2)
            emit_chunk(gc)
        if NGC > 1:
            emit_out(NGC - 2)
            after_out(NGC - 2)
        emit_out(NGC - 1)

        v_final = p.cnt("V")

        p.op("sync", lambda e: e.dma_start(out=outd, in_=T_x[:, :]),
             waits=[("V", v_final)], inc="dO", inc_n=16)
        p.op("sync", lambda e: e.wait_ge(sems["dO"], 16))

        with nc.Block() as block:
            def runner(eng_name):
                def body(eng):
                    for w, fn, inc, inc_n in p.ops[eng_name]:
                        for s_name, v in sorted(w.items()):
                            eng.wait_ge(sems[s_name], v)
                        ins = fn(eng)
                        if inc is not None and ins is not None:
                            ins.then_inc(sems[inc], inc_n)
                return body
            block.tensor(runner("tensor"))
            block.vector(runner("vector"))
            block.scalar(runner("scalar"))
            block.sync(runner("sync"))
            block.gpsimd(runner("gpsimd"))

    return nc


_NC_CACHE = {}


def get_nc(NT, E):
    key = (NT, E)
    if key not in _NC_CACHE:
        _NC_CACHE[key] = build_nc(NT, E)
    return _NC_CACHE[key]


def make_shared(Wq, bq, gate_w, w1, w3, w2, e_bias, NT, E):
    f = np.float32
    wq_p = np.ascontiguousarray(
        np.asarray(Wq, f).reshape(NH, 128, H).transpose(1, 0, 2)).reshape(128, NH * H)
    bq_p = np.asarray(bq, f).reshape(1, H)
    gw_p = np.ascontiguousarray(
        np.asarray(gate_w, f).T.reshape(NH, 128, E).transpose(1, 0, 2)).reshape(128, NH * E)
    eb_p = np.ascontiguousarray(np.broadcast_to(
        np.tile(np.asarray(e_bias, f).reshape(1, E), (1, NT)), (128, NT * E)))
    w1_p = np.ascontiguousarray(
        np.asarray(w1, f).reshape(E, NH, 128, I_DIM).transpose(0, 2, 1, 3)).reshape(E, 128, NH * I_DIM)
    w3_p = np.ascontiguousarray(
        np.asarray(w3, f).reshape(E, NH, 128, I_DIM).transpose(0, 2, 1, 3)).reshape(E, 128, NH * I_DIM)
    w2_p = np.ascontiguousarray(
        np.asarray(w2, f).reshape(E, NI, 128, H).transpose(0, 2, 1, 3)).reshape(E, 128, NI * H)
    id_f = np.eye(128, dtype=f)
    id_b = np.eye(128, dtype=ml_dtypes.bfloat16)
    return {"wq": wq_p, "bqd": bq_p, "gwd": gw_p, "ebd": eb_p,
            "idfd": id_f, "idbd": id_b, "w1d": w1_p, "w3d": w3_p, "w2d": w2_p}


def kernel(L0_attn_inputs, L0_attn_Wq, L0_attn_bq, L0_ffn_gate_w,
           L0_ffn_w1, L0_ffn_w3, L0_ffn_w2, L0_ffn_e_bias):
    NT, E = 16, 32
    x = np.asarray(L0_attn_inputs, np.float32)
    B = x.shape[0]
    nc = get_nc(NT, E)
    shared = make_shared(L0_attn_Wq, L0_attn_bq, L0_ffn_gate_w,
                         L0_ffn_w1, L0_ffn_w3, L0_ffn_w2, L0_ffn_e_bias, NT, E)
    in_maps = []
    for b in range(B):
        xp = np.ascontiguousarray(
            x[b].reshape(NT, 128, H).transpose(1, 0, 2)).reshape(128, NT * H)
        in_maps.append({"xp": xp, **shared})
    res = run_bass_kernel_spmd(nc, in_maps, core_ids=list(range(B)))
    outs = [r["out"].reshape(128, NT, H).transpose(1, 0, 2).reshape(NT * 128, H)
            for r in res.results]
    out = np.stack(outs)
    if out.dtype != np.float64:
        out = out.astype(np.float32)
    return out


# version 05
# speedup vs baseline: 1.1339x; 1.0096x over previous
import numpy as np
import ml_dtypes
from contextlib import ExitStack

import concourse.bass as bass
import concourse.mybir as mybir
from concourse.bass_utils import run_bass_kernel_spmd

F32 = mybir.dt.float32
BF16 = mybir.dt.bfloat16
AF = mybir.ActivationFunctionType
ALU = mybir.AluOpType
AX = mybir.AxisListType

H = 1024
NH = 8
I_DIM = 768
NI = 6
KTOP = 4
EPS = 1e-5
SCALE = 1.0 / 32.0


class Plan:
    ENGINES = ("tensor", "vector", "scalar", "sync", "gpsimd")

    def __init__(self):
        self.ops = {e: [] for e in self.ENGINES}
        self.n = {}
        self.last_wait = {e: {} for e in self.ENGINES}

    def cnt(self, sem):
        return self.n.get(sem, 0)

    def op(self, eng, fn, waits=(), inc=None, inc_n=1):
        w = {}
        for s, v in waits:
            if v is None or v <= 0:
                continue
            if self.last_wait[eng].get(s, 0) >= v:
                continue
            w[s] = max(w.get(s, 0), v)
        for s, v in w.items():
            self.last_wait[eng][s] = v
        tok = None
        if inc is not None:
            self.n[inc] = self.n.get(inc, 0) + inc_n
            tok = self.n[inc]
        self.ops[eng].append((w, fn, inc, inc_n))
        return tok

    def v(self, fn, waits=()):
        return self.op("vector", fn, waits=[("V", self.cnt("V"))] + list(waits), inc="V")

    def s(self, fn, waits=()):
        return self.op("scalar", fn, waits=[("S", self.cnt("S"))] + list(waits), inc="S")

    def t(self, fn, waits=(), inc="T"):
        return self.op("tensor", fn, waits=waits, inc=inc)


def bcast_inner(ap2d, seg, rep):
    steps = [list(p) for p in ap2d.ap[:-1]]
    return bass.AP(ap2d.tensor, ap2d.offset, steps + [[1, seg], [0, rep]])


def strided_cols(handle, off, step, cnt):
    ap = handle[:, :]
    steps = [list(ap.ap[0])] + [[step, cnt]]
    return bass.AP(ap.tensor, off, steps)


def build_nc(NT, E):
    T = NT * 128
    NQ = max(1, T // 512)
    NCH = max(1, T // 512)
    NE_COL = NT * E
    NG = (NT + 3) // 4

    nc = bass.Bass("TRN2", target_bir_lowering=False)

    NXC = min(4, NT)
    xc_tiles = NT // NXC
    xp = nc.dram_tensor("xp", [128, NT * H], F32, kind="ExternalInput").ap()
    wq = nc.dram_tensor("wq", [128, NH * H], F32, kind="ExternalInput").ap()
    bqd = nc.dram_tensor("bqd", [1, H], F32, kind="ExternalInput").ap()
    gwd = nc.dram_tensor("gwd", [128, NH * E], F32, kind="ExternalInput").ap()
    ebd = nc.dram_tensor("ebd", [128, NE_COL], F32, kind="ExternalInput").ap()
    idfd = nc.dram_tensor("idfd", [128, 128], F32, kind="ExternalInput").ap()
    idbd = nc.dram_tensor("idbd", [128, 128], BF16, kind="ExternalInput").ap()
    w1d = nc.dram_tensor("w1d", [E, 128, NH * I_DIM], F32, kind="ExternalInput").ap()
    w3d = nc.dram_tensor("w3d", [E, 128, NH * I_DIM], F32, kind="ExternalInput").ap()
    w2d = nc.dram_tensor("w2d", [E, 128, NI * H], F32, kind="ExternalInput").ap()
    outd = nc.dram_tensor("out", [128, NT * H], F32, kind="ExternalOutput").ap()

    A_LN1T = 0
    A_WQ = NT * 1024
    A_SIZE = max(A_WQ + NH * H, NT * T, 32768)
    WB = NH * I_DIM
    B0_W1, B0_W3, B0_W2 = 0, WB, 2 * WB
    B1_W1, B1_W3 = 3 * WB, 4 * WB
    B1_W2A = 5 * WB
    B1_W2A_LEN = A_SIZE - B1_W2A
    assert A_SIZE >= B1_W2A
    B1_W2B_LEN = WB - B1_W2A_LEN

    B_AT = 4096
    B_S = B_AT + 2 * NI * 512
    B_SIZE = max(NT * 1024, B_S + 2 * NI * 512)
    B_MACC = B_AT
    B_MSK = B_AT + NE_COL

    ST_BN = 0
    ST_MV = 24
    ST_RSTD = 24 + 2 * NT
    ST_NEGMUR = ST_RSTD + NT
    ST_ALPHA = ST_NEGMUR + NT
    ST_Z4 = ST_ALPHA + NT
    ST_ZSUM = ST_Z4 + 2 * NQ
    ST_RMAX = ST_ZSUM + 2
    ST_DEN = ST_RMAX + NT
    ST_RCP = ST_DEN + NT
    ST_TMP = ST_RCP + NT
    ST_SIZE = ST_TMP + 12
    assert ST_SIZE <= 192

    st = ExitStack()
    with st:
        T_x = st.enter_context(nc.sbuf_tensor("T_x", [128, NT * H], F32))
        T_fm = st.enter_context(nc.sbuf_tensor("T_fm", [128, NT * 1024], BF16))
        T_A = st.enter_context(nc.sbuf_tensor("T_A", [128, A_SIZE], BF16))
        T_B = st.enter_context(nc.sbuf_tensor("T_B", [128, B_SIZE], BF16))
        T_st = st.enter_context(nc.sbuf_tensor("T_st", [128, 192], F32))
        T_c = st.enter_context(nc.sbuf_tensor("T_c", [128, NE_COL], F32))
        T_lg = st.enter_context(nc.sbuf_tensor("T_lg", [128, NE_COL], F32))
        T_eb = st.enter_context(nc.sbuf_tensor("T_eb", [128, NE_COL], F32))
        T_gw = st.enter_context(nc.sbuf_tensor("T_gw", [128, NH * E], F32))
        T_x2t = st.enter_context(nc.sbuf_tensor("T_x2t", [128, 1024], F32))
        T_srep = st.enter_context(nc.sbuf_tensor("T_srep", [128, E], F32))
        T_ones = st.enter_context(nc.sbuf_tensor("T_ones", [128, 128], F32))
        bq_bf = st.enter_context(nc.sbuf_tensor("bq_bf", [1, H], BF16))
        ones_b = st.enter_context(nc.sbuf_tensor("ones_b", [1, 128], BF16))
        id_b = st.enter_context(nc.sbuf_tensor("id_b", [128, 128], BF16))
        id_f = st.enter_context(nc.sbuf_tensor("id_f", [128, 128], F32))

        P0 = st.enter_context(nc.psum_tensor("P0", [128, 2048], F32))
        P1 = st.enter_context(nc.psum_tensor("P1", [128, 1024], F32))
        PTB = st.enter_context(nc.psum_tensor("PTB", [128, 1024], BF16))
        PTB2 = st.enter_context(nc.psum_tensor("PTB2", [128, 1024], BF16))

        sems = {}
        sem_names = ["dW", "dA", "dB", "dO", "dI", "dG", "T", "S", "V"] + \
            [f"dX{i}" for i in range(NXC)]
        for name in sem_names:
            sems[name] = st.enter_context(nc.semaphore(name))

        p = Plan()

        def d(fn, eng="sync", waits=(), inc="dX0"):
            return p.op(eng, fn, waits=waits, inc=inc, inc_n=16)

        d(lambda e: e.dma_start(out=T_x[:, 0:xc_tiles * H],
                                in_=xp[:, 0:xc_tiles * H]), inc="dX0")
        d(lambda e: e.dma_start(out=id_b[:, :], in_=idbd), inc="dI")
        d(lambda e: e.dma_start(out=id_f[:, :], in_=idfd), inc="dI")
        for ci in range(1, NXC):
            d(lambda e, ci=ci: e.dma_start(
                out=T_x[:, ci * xc_tiles * H:(ci + 1) * xc_tiles * H],
                in_=xp[:, ci * xc_tiles * H:(ci + 1) * xc_tiles * H]),
                inc=f"dX{ci}")
        d(lambda e: e.dma_start(out=T_gw[:, :], in_=gwd), inc="dG")
        d(lambda e: e.dma_start(out=T_eb[:, :], in_=ebd), inc="dG")
        dG_all = p.cnt("dG")

        d(lambda e: e.dma_start(out=T_A[:, A_WQ:A_WQ + NH * H], in_=wq),
          eng="gpsimd", inc="dW")
        d(lambda e: e.dma_start(out=bq_bf[:, :], in_=bqd), eng="gpsimd", inc="dW")
        dW_all = p.cnt("dW")

        v_ones_b = p.v(lambda e: e.memset(ones_b[:, :], 1.0))
        v_ones_m = p.v(lambda e: e.memset(T_ones[:, :], 1.0))

        def col(c0, n=1):
            return T_st[:, c0:c0 + n]

        def emit_ln_stats_group(g, x_of, xwaits_of):
            tiles = list(range(4 * g, min(4 * g + 4, NT)))
            for j in tiles:
                sl = ST_BN + (j % 2) * 12
                x_ap = x_of(j)
                p.v(lambda e, a=x_ap, sl=sl: e.bn_stats(col(sl, 6), a[:, 0:512]),
                    waits=xwaits_of(j))
                p.v(lambda e, a=x_ap, sl=sl: e.bn_stats(col(sl + 6, 6), a[:, 512:1024]))
                p.v(lambda e, j=j, sl=sl: e.bn_aggr(col(ST_MV + 2 * j, 2), col(sl, 12)))
            nt_g = len(tiles)
            j0 = tiles[0]
            var_s = strided_cols(T_st, ST_MV + 2 * j0 + 1, 2, nt_g)
            mean_s = strided_cols(T_st, ST_MV + 2 * j0, 2, nt_g)
            rstd = col(ST_RSTD + j0, nt_g)
            negmur = col(ST_NEGMUR + j0, nt_g)
            t_y2 = col(ST_TMP, nt_g)
            t_y2v = col(ST_TMP + 4, nt_g)
            t_f = col(ST_TMP + 8, nt_g)
            vv = p.v(lambda e: e.tensor_scalar_add(var_s, var_s, EPS))
            ts = p.s(lambda e: e.activation(t_y2, var_s, AF.Sqrt), waits=[("V", vv)])
            p.v(lambda e: e.reciprocal(rstd, t_y2), waits=[("S", ts)])
            p.v(lambda e: e.tensor_tensor(t_y2, rstd, rstd, ALU.mult))
            p.v(lambda e: e.tensor_tensor(t_y2v, t_y2, var_s, ALU.mult))
            p.v(lambda e: e.tensor_scalar(t_f, t_y2v, -0.5, 1.5, ALU.mult, ALU.add))
            p.v(lambda e: e.tensor_tensor(rstd, rstd, t_f, ALU.mult))
            p.v(lambda e: e.scalar_tensor_tensor(negmur, mean_s, -1.0, rstd,
                                                 ALU.mult, ALU.mult))

        def emit_ln_apply(j, x_ap, lnf_ap, extra=()):
            return p.v(lambda e, j=j, o=lnf_ap, a=x_ap: e.tensor_scalar(
                o, a, col(ST_RSTD + j), col(ST_NEGMUR + j), ALU.mult, ALU.add),
                waits=list(extra))

        def emit_ln_group(g, x_of, lnf_of, xwaits_of):
            emit_ln_stats_group(g, x_of, xwaits_of)
            return [emit_ln_apply(j, x_of(j), lnf_of(j))
                    for j in range(4 * g, min(4 * g + 4, NT))]

        def fmv(handle, j):
            return handle[:, 0:NT * 1024].rearrange(
                "p (c t) -> p c t", c=NH)[:, :, j * 128:(j + 1) * 128]

        s_ev_ln1 = [None] * NT
        t_q = [None] * NT
        s_ev_qtok = [None] * NT
        t_tr2 = [None] * NT
        s_ev_qt = [None] * NT
        t_tr1 = [None] * NT
        lnf_toks = [None] * NT

        for g in range(NG):
            toks = emit_ln_group(
                g,
                x_of=lambda j: T_x[:, j * H:(j + 1) * H],
                lnf_of=lambda j: T_B[:, j * 1024:(j + 1) * 1024],
                xwaits_of=lambda j: [(f"dX{j // xc_tiles}", 16)])
            for i, j in enumerate(range(4 * g, min(4 * g + 4, NT))):
                lnf_toks[j] = toks[i]

        def p1_tr_ln(j):
            def tr_ln(e, j=j):
                for k in range(NH):
                    ins = e.transpose(
                        PTB[:, k * 128:(k + 1) * 128],
                        T_B[:, j * 1024 + k * 128: j * 1024 + k * 128 + 128],
                        id_b[:, :])
                return ins
            t_tr1[j] = p.t(tr_ln, waits=[("V", lnf_toks[j]), ("dI", 32),
                                         ("S", s_ev_ln1[j - 1] if j else 0)])
            s_ev_ln1[j] = p.s(lambda e, j=j: e.copy(
                fmv(T_A, j),
                PTB[:, :].rearrange("p (c t) -> p c t", c=NH)),
                waits=[("T", t_tr1[j])])

        def p1_q(j):
            def qmm(e, j=j):
                base = (j % 2) * 1024
                for half in range(2):
                    for hc in range(NH):
                        e.matmul(P0[:, base + half * 512: base + half * 512 + 512],
                                 T_A[:, A_LN1T + hc * (NT * 128) + j * 128:
                                     A_LN1T + hc * (NT * 128) + j * 128 + 128],
                                 T_A[:, A_WQ + hc * H + half * 512:
                                     A_WQ + hc * H + half * 512 + 512],
                                 start=(hc == 0), stop=False)
                    ins = e.matmul(P0[:, base + half * 512: base + half * 512 + 512],
                                   ones_b[:, :],
                                   bq_bf[:, half * 512: half * 512 + 512],
                                   start=False, stop=True)
                return ins
            t_q[j] = p.t(qmm, waits=[("S", s_ev_ln1[j]), ("dW", dW_all),
                                     ("V", v_ones_b),
                                     ("S", s_ev_qtok[j - 2] if j >= 2 else 0)])
            s_ev_qtok[j] = p.s(lambda e, j=j: e.copy(
                T_B[:, j * 1024:(j + 1) * 1024],
                P0[:, (j % 2) * 1024:(j % 2) * 1024 + 1024]),
                waits=[("T", t_q[j])])

        def p1_tr_q(j):
            def tr_q(e, j=j):
                for k in range(NH):
                    ins = e.transpose(
                        PTB2[:, k * 128:(k + 1) * 128],
                        T_B[:, j * 1024 + k * 128: j * 1024 + k * 128 + 128],
                        id_b[:, :])
                return ins
            t_tr2[j] = p.t(tr_q, waits=[("S", s_ev_qtok[j]),
                                        ("S", s_ev_qt[j - 1] if j else 0)])
            s_ev_qt[j] = p.s(lambda e, j=j: e.copy(
                fmv(T_fm, j),
                PTB2[:, :].rearrange("p (c t) -> p c t", c=NH)),
                waits=[("T", t_tr2[j])])

        for k in range(NT + 2):
            if k < NT:
                p1_tr_ln(k)
            if 0 <= k - 1 < NT:
                p1_q(k - 1)
            if 0 <= k - 2 < NT:
                p1_tr_q(k - 2)

        t_sq = [[None] * NQ for _ in range(NT)]
        s_exp = [[None] * NQ for _ in range(NT)]
        v_alpha = [None] * NT
        for j in range(NT):
            for q in range(NQ):
                def smm(e, j=j, q=q):
                    for dc in range(NH):
                        ins = e.matmul(
                            P0[:, q * 512:(q + 1) * 512],
                            T_fm[:, dc * (NT * 128) + j * 128:
                                 dc * (NT * 128) + j * 128 + 128],
                            T_fm[:, dc * (NT * 128) + q * 512:
                                 dc * (NT * 128) + q * 512 + 512],
                            start=(dc == 0), stop=(dc == NH - 1))
                    return ins
                t_sq[j][q] = p.t(smm, waits=[
                    ("S", s_ev_qt[NT - 1]),
                    ("S", s_exp[j - 1][q] if j else 0)])
                s_exp[j][q] = p.s(lambda e, j=j, q=q: e.activation(
                    T_A[:, j * T + q * 512: j * T + q * 512 + 512],
                    P0[:, q * 512:(q + 1) * 512], AF.Exp, scale=SCALE),
                    waits=[("T", t_sq[j][q])])
            zsum = col(ST_ZSUM + (j % 2))
            vz = p.v(lambda e, j=j, o=zsum: e.reduce_sum(
                o, T_A[:, j * T: j * T + T], axis=AX.X),
                waits=[("S", s_exp[j][NQ - 1])])
            v_alpha[j] = p.v(lambda e, j=j, zs=zsum: e.reciprocal(
                col(ST_ALPHA + j), zs))

        t_ao = [[None] * 2 for _ in range(NT)]
        v_ev3 = [[None] * 2 for _ in range(NT)]
        s_exp_all = p.cnt("S")
        for j in range(NT):
            for hh in range(2):
                slot = (2 * j + hh) % 4
                def aomm(e, j=j, hh=hh, slot=slot):
                    for i in range(NT):
                        ins = e.matmul(
                            P0[:, slot * 512: slot * 512 + 512],
                            T_A[:, i * T + j * 128: i * T + j * 128 + 128],
                            T_B[:, i * 1024 + hh * 512: i * 1024 + hh * 512 + 512],
                            start=(i == 0), stop=(i == NT - 1))
                    return ins
                prev = v_ev3[(2 * j + hh - 4) // 2][(2 * j + hh - 4) % 2] \
                    if 2 * j + hh >= 4 else 0
                t_ao[j][hh] = p.t(aomm, waits=[("S", s_exp_all), ("V", prev)])
                v_ev3[j][hh] = p.v(lambda e, j=j, hh=hh, slot=slot: e.scalar_tensor_tensor(
                    T_x[:, j * H + hh * 512: j * H + hh * 512 + 512],
                    P0[:, slot * 512: slot * 512 + 512],
                    col(ST_ALPHA + j),
                    T_x[:, j * H + hh * 512: j * H + hh * 512 + 512],
                    ALU.mult, ALU.add),
                    waits=[("T", t_ao[j][hh]), ("V", v_alpha[j])])
            if j % 4 == 3:
                emit_ln_stats_group(
                    j // 4,
                    x_of=lambda jj: T_x[:, jj * H:(jj + 1) * H],
                    xwaits_of=lambda jj: [("V", v_ev3[jj][1])])

        t_p3_all = p.cnt("T")
        v_p3_all = p.cnt("V")

        def srep_mm(e):
            for hc in range(NH):
                ins = e.matmul(P0[:, 512:512 + E], T_ones[:, :],
                               T_gw[:, hc * E:(hc + 1) * E],
                               start=(hc == 0), stop=(hc == NH - 1))
            return ins
        t_srep = p.t(srep_mm, waits=[("V", v_ones_m), ("dG", dG_all),
                                     ("V", v_p3_all)])
        v_srep = p.v(lambda e: e.tensor_copy(T_srep[:, :], P0[:, 512:512 + E]),
                     waits=[("T", t_srep)])

        s_ev_ln2 = [None] * NT
        s_x2t = [None] * NT
        t_x2tr = [None] * NT
        t_tr4 = [None] * NT
        t_rt = [None] * NT
        s_lgev = [None] * NT
        v_fix = [None] * NT
        lnf2_toks = [None] * NT

        def emit_fixups(tiles):
            for j in tiles:
                p.v(lambda e, j=j: e.tensor_scalar_mul(
                    T_lg[:, j * E:(j + 1) * E], T_lg[:, j * E:(j + 1) * E],
                    col(ST_RSTD + j)),
                    waits=[("S", s_lgev[j])])
                v_fix[j] = p.v(lambda e, j=j: e.scalar_tensor_tensor(
                    T_lg[:, j * E:(j + 1) * E], T_srep[:, :],
                    col(ST_NEGMUR + j), T_lg[:, j * E:(j + 1) * E],
                    ALU.mult, ALU.add))

        for j in range(NT):
            lnf2_toks[j] = emit_ln_apply(
                j, T_x[:, j * H:(j + 1) * H], T_B[:, j * 1024:(j + 1) * 1024],
                extra=([("T", t_p3_all), ("V", v_srep)] if j == 0 else []))

        def p4_x2tr(j):
            def x2tr(e, j=j):
                for k in range(NH):
                    ins = e.transpose(
                        P1[:, k * 128:(k + 1) * 128],
                        T_x[:, j * H + k * 128: j * H + k * 128 + 128],
                        id_f[:, :])
                return ins
            t_x2tr[j] = p.t(x2tr, waits=[("V", v_ev3[j][1]),
                                         ("S", s_x2t[j - 1] if j else 0)])

        def p4_lnrt(j):
            def tr_ln2(e, j=j):
                for k in range(NH):
                    ins = e.transpose(
                        PTB[:, k * 128:(k + 1) * 128],
                        T_B[:, j * 1024 + k * 128: j * 1024 + k * 128 + 128],
                        id_b[:, :])
                return ins
            t_tr4[j] = p.t(tr_ln2, waits=[("V", lnf2_toks[j]),
                                          ("S", s_ev_ln2[j - 1] if j else 0)])

            def rtmm(e, j=j):
                for hc in range(NH):
                    ins = e.matmul(P0[:, 0:E],
                                   T_x2t[:, hc * 128:(hc + 1) * 128],
                                   T_gw[:, hc * E:(hc + 1) * E],
                                   start=(hc == 0), stop=(hc == NH - 1))
                return ins
            t_rt[j] = p.t(rtmm, waits=[("S", s_x2t[j]), ("dG", dG_all),
                                       ("S", s_lgev[j - 1] if j else 0)])

        for k in range(NT + 1):
            if k < NT:
                p4_x2tr(k)
            if 0 <= k - 1 < NT:
                p4_lnrt(k - 1)
            if k < NT:
                s_x2t[k] = p.s(lambda e, k=k: e.copy(T_x2t[:, :], P1[:, :]),
                               waits=[("T", t_x2tr[k]),
                                      ("T", t_rt[k - 1] if k else 0)])
            if 0 <= k - 1 < NT:
                jj = k - 1
                s_ev_ln2[jj] = p.s(lambda e, j=jj: e.copy(
                    fmv(T_fm, j),
                    PTB[:, :].rearrange("p (c t) -> p c t", c=NH)),
                    waits=[("T", t_tr4[jj])])
                s_lgev[jj] = p.s(lambda e, j=jj: e.copy(
                    T_lg[:, j * E:(j + 1) * E], P0[:, 0:E]),
                    waits=[("T", t_rt[jj])])
        emit_fixups(range(NT))

        macc = T_B[:, B_MACC:B_MACC + NE_COL]
        msk = T_B[:, B_MSK:B_MSK + NE_COL]
        rmax = col(ST_RMAX, NT)
        den = col(ST_DEN, NT)
        rcpd = col(ST_RCP, NT)

        s_sig = p.s(lambda e: e.activation(T_lg[:, :], T_lg[:, :], AF.Sigmoid),
                    waits=[("V", v_fix[NT - 1])])
        d3 = lambda ap: ap.rearrange("p (s e) -> p s e", s=NT)
        p.v(lambda e: e.tensor_add(T_eb[:, :], T_lg[:, :], T_eb[:, :]),
            waits=[("S", s_sig)])
        for k in range(KTOP):
            p.v(lambda e: e.reduce_max(
                rmax.rearrange("p (s o) -> p s o", s=NT), d3(T_eb[:, :]), axis=AX.X))
            p.v(lambda e: e.tensor_tensor(
                d3(msk), d3(T_eb[:, :]), bcast_inner(rmax, NT, E), ALU.is_ge))
            if k == 0:
                p.v(lambda e: e.tensor_copy(macc, msk))
            else:
                p.v(lambda e: e.tensor_add(macc, macc, msk))
            if k < KTOP - 1:
                p.v(lambda e: e.scalar_tensor_tensor(
                    T_eb[:, :], msk, -1e30, T_eb[:, :], ALU.mult, ALU.add))
        p.v(lambda e: e.tensor_tensor(T_lg[:, :], T_lg[:, :], macc, ALU.mult))
        p.v(lambda e: e.reduce_sum(
            den.rearrange("p (s o) -> p s o", s=NT), d3(T_lg[:, :]), axis=AX.X))
        p.v(lambda e: e.reciprocal(rcpd, den))
        v_c = p.v(lambda e: e.tensor_tensor(
            d3(T_c[:, :]), d3(T_lg[:, :]), bcast_inner(rcpd, NT, E), ALU.mult))

        s_ln2_all = s_lgev[NT - 1]

        def w1_ap(par, hc, ic):
            off = (B0_W1 if par == 0 else B1_W1) + hc * I_DIM + ic * 128
            return T_A[:, off: off + 128]

        def w3_ap(par, hc, ic):
            off = (B0_W3 if par == 0 else B1_W3) + hc * I_DIM + ic * 128
            return T_A[:, off: off + 128]

        def w2_ap(par, ic, hh):
            off = ic * H + hh * 512
            if par == 0:
                return T_A[:, B0_W2 + off: B0_W2 + off + 512]
            if off + 512 <= B1_W2A_LEN:
                return T_A[:, B1_W2A + off: B1_W2A + off + 512]
            off -= B1_W2A_LEN
            return T_B[:, off: off + 512]

        def at_ap(par, ic, sl=None):
            off = B_AT + par * (NI * 512) + ic * 512
            if sl is None:
                return T_B[:, off: off + 512]
            return T_B[:, off + sl[0]: off + sl[1]]

        def s_ap(par, ic):
            off = B_S + par * (NI * 512) + ic * 512
            return T_B[:, off: off + 512]

        exp_dma_tok = [None] * E
        t_out_done = [None] * E

        def issue_expert(e_idx, gate_toks):
            par = e_idx % 2
            sem = "dA" if par == 0 else "dB"
            p.op("gpsimd", lambda e, i=e_idx, par=par: e.dma_start(
                out=T_A[:, (B0_W1 if par == 0 else B1_W1):
                        (B0_W1 if par == 0 else B1_W1) + WB],
                in_=w1d[i, :, :]), waits=gate_toks, inc=sem, inc_n=16)
            p.op("gpsimd", lambda e, i=e_idx, par=par: e.dma_start(
                out=T_A[:, (B0_W3 if par == 0 else B1_W3):
                        (B0_W3 if par == 0 else B1_W3) + WB],
                in_=w3d[i, :, :]), inc=sem, inc_n=16)
            if par == 0:
                p.op("gpsimd", lambda e, i=e_idx: e.dma_start(
                    out=T_A[:, B0_W2: B0_W2 + WB], in_=w2d[i, :, :]),
                    inc=sem, inc_n=16)
            else:
                p.op("gpsimd", lambda e, i=e_idx: e.dma_start(
                    out=T_A[:, B1_W2A: B1_W2A + B1_W2A_LEN],
                    in_=w2d[i, :, 0:B1_W2A_LEN]), inc=sem, inc_n=16)
                p.op("gpsimd", lambda e, i=e_idx: e.dma_start(
                    out=T_B[:, 0:B1_W2B_LEN],
                    in_=w2d[i, :, B1_W2A_LEN:WB]), inc=sem, inc_n=16)
            exp_dma_tok[e_idx] = (sem, p.cnt(sem))

        issue_expert(0, [("T", t_p3_all)])
        if E > 1:
            issue_expert(1, [("T", t_tr4[min(3, NT - 1)])])

        NGC = E * NCH
        t_g = {}
        t_u = {}
        s_silu = {}
        v_a = {}
        t_og = {}
        v_oev = {}
        out_slots = [P1[:, 0:512], P1[:, 512:1024]]
        oslot_war = {0: ("S", s_x2t[NT - 1]), 1: ("S", s_x2t[NT - 1])}
        bank_war = {0: ("S", s_ln2_all), 1: ("V", v_srep),
                    2: ("V", v_p3_all), 3: ("V", v_p3_all)}

        def emit_chunk(gc):
            e_i, c = divmod(gc, NCH)
            par = e_i % 2
            gpar = gc % 2
            sem, dval = exp_dma_tok[e_i]
            for ic in range(NI):
                gb, ub = ic % 2, 2 + ic % 2

                def gmm(e, c=c, par=par, ic=ic, gb=gb):
                    for hc in range(NH):
                        ins = e.matmul(
                            P0[:, gb * 512: gb * 512 + 512],
                            w1_ap(par, hc, ic),
                            T_fm[:, hc * (NT * 128) + c * 512:
                                 hc * (NT * 128) + c * 512 + 512],
                            start=(hc == 0), stop=(hc == NH - 1))
                    return ins
                t_g[(gc, ic)] = p.t(gmm, waits=[(sem, dval), ("S", s_ln2_all),
                                                bank_war[gb]])

                def umm(e, c=c, par=par, ic=ic, ub=ub):
                    for hc in range(NH):
                        ins = e.matmul(
                            P0[:, ub * 512: ub * 512 + 512],
                            w3_ap(par, hc, ic),
                            T_fm[:, hc * (NT * 128) + c * 512:
                                 hc * (NT * 128) + c * 512 + 512],
                            start=(hc == 0), stop=(hc == NH - 1))
                    return ins
                t_u[(gc, ic)] = p.t(umm, waits=[bank_war[ub]])

                w = [("T", t_g[(gc, ic)])]
                if gc >= 2:
                    w.append(("V", v_a[(gc - 2, ic)]))
                s_silu[(gc, ic)] = p.s(lambda e, gpar=gpar, ic=ic, gb=gb: e.activation(
                    s_ap(gpar, ic), P0[:, gb * 512: gb * 512 + 512], AF.Silu),
                    waits=w)
                bank_war[gb] = ("S", s_silu[(gc, ic)])
                w = [("T", t_u[(gc, ic)]), ("S", s_silu[(gc, ic)])]
                if gc >= 2:
                    w.append(("T", t_og[(gc - 2, 7)]))
                if gc < 2:
                    w.append(("V", v_c))
                v_a[(gc, ic)] = p.v(lambda e, gpar=gpar, ic=ic, ub=ub: e.tensor_tensor(
                    at_ap(gpar, ic),
                    s_ap(gpar, ic),
                    P0[:, ub * 512: ub * 512 + 512], ALU.mult),
                    waits=w)
                bank_war[ub] = ("V", v_a[(gc, ic)])

        def emit_out(gc):
            e_i, c = divmod(gc, NCH)
            par = e_i % 2
            gpar = gc % 2
            sem, dval = exp_dma_tok[e_i]
            grp = 0
            for ts in range(4):
                for hh in range(2):
                    slot = grp % 2
                    def omm(e, par=par, gpar=gpar, ts=ts, hh=hh, slot=slot):
                        for ic in range(NI):
                            ins = e.matmul(
                                out_slots[slot],
                                at_ap(gpar, ic, (ts * 128, ts * 128 + 128)),
                                w2_ap(par, ic, hh),
                                start=(ic == 0), stop=(ic == NI - 1))
                        return ins
                    w = [("V", v_a[(gc, NI - 1)]), (sem, dval), oslot_war[slot]]
                    t_og[(gc, grp)] = p.t(omm, waits=w)
                    tile = c * 4 + ts
                    v_oev[(gc, grp)] = p.v(
                        lambda e, e_i=e_i, ts=ts, hh=hh, slot=slot, tile=tile:
                        e.scalar_tensor_tensor(
                            T_x[:, tile * H + hh * 512: tile * H + hh * 512 + 512],
                            out_slots[slot],
                            T_c[:, tile * E + e_i: tile * E + e_i + 1],
                            T_x[:, tile * H + hh * 512: tile * H + hh * 512 + 512],
                            ALU.mult, ALU.add),
                        waits=[("T", t_og[(gc, grp)]), ("V", v_c)])
                    oslot_war[slot] = ("V", v_oev[(gc, grp)])
                    grp += 1
            if c == NCH - 1:
                t_out_done[e_i] = p.cnt("T")

        def after_out(x):
            e_done, c_done = divmod(x, NCH)
            if c_done == NCH - 1 and e_done + 2 < E:
                issue_expert(e_done + 2, [("T", t_out_done[e_done])])

        emit_chunk(0)
        if NGC > 1:
            emit_chunk(1)
        for gc in range(2, NGC):
            emit_out(gc - 2)
            after_out(gc - 2)
            emit_chunk(gc)
        if NGC > 1:
            emit_out(NGC - 2)
            after_out(NGC - 2)
        emit_out(NGC - 1)

        v_final = p.cnt("V")

        p.op("sync", lambda e: e.dma_start(out=outd, in_=T_x[:, :]),
             waits=[("V", v_final)], inc="dO", inc_n=16)
        p.op("sync", lambda e: e.wait_ge(sems["dO"], 16))

        with nc.Block() as block:
            def runner(eng_name):
                def body(eng):
                    for w, fn, inc, inc_n in p.ops[eng_name]:
                        for s_name, v in sorted(w.items()):
                            eng.wait_ge(sems[s_name], v)
                        ins = fn(eng)
                        if inc is not None and ins is not None:
                            ins.then_inc(sems[inc], inc_n)
                return body
            block.tensor(runner("tensor"))
            block.vector(runner("vector"))
            block.scalar(runner("scalar"))
            block.sync(runner("sync"))
            block.gpsimd(runner("gpsimd"))

    return nc


_NC_CACHE = {}


def get_nc(NT, E):
    key = (NT, E)
    if key not in _NC_CACHE:
        _NC_CACHE[key] = build_nc(NT, E)
    return _NC_CACHE[key]


def make_shared(Wq, bq, gate_w, w1, w3, w2, e_bias, NT, E):
    f = np.float32
    wq_p = np.ascontiguousarray(
        np.asarray(Wq, f).reshape(NH, 128, H).transpose(1, 0, 2)).reshape(128, NH * H)
    bq_p = np.asarray(bq, f).reshape(1, H)
    gw_p = np.ascontiguousarray(
        np.asarray(gate_w, f).T.reshape(NH, 128, E).transpose(1, 0, 2)).reshape(128, NH * E)
    eb_p = np.ascontiguousarray(np.broadcast_to(
        np.tile(np.asarray(e_bias, f).reshape(1, E), (1, NT)), (128, NT * E)))
    w1_p = np.ascontiguousarray(
        np.asarray(w1, f).reshape(E, NH, 128, I_DIM).transpose(0, 2, 1, 3)).reshape(E, 128, NH * I_DIM)
    w3_p = np.ascontiguousarray(
        np.asarray(w3, f).reshape(E, NH, 128, I_DIM).transpose(0, 2, 1, 3)).reshape(E, 128, NH * I_DIM)
    w2_p = np.ascontiguousarray(
        np.asarray(w2, f).reshape(E, NI, 128, H).transpose(0, 2, 1, 3)).reshape(E, 128, NI * H)
    id_f = np.eye(128, dtype=f)
    id_b = np.eye(128, dtype=ml_dtypes.bfloat16)
    return {"wq": wq_p, "bqd": bq_p, "gwd": gw_p, "ebd": eb_p,
            "idfd": id_f, "idbd": id_b, "w1d": w1_p, "w3d": w3_p, "w2d": w2_p}


def kernel(L0_attn_inputs, L0_attn_Wq, L0_attn_bq, L0_ffn_gate_w,
           L0_ffn_w1, L0_ffn_w3, L0_ffn_w2, L0_ffn_e_bias):
    NT, E = 16, 32
    x = np.asarray(L0_attn_inputs, np.float32)
    B = x.shape[0]
    nc = get_nc(NT, E)
    shared = make_shared(L0_attn_Wq, L0_attn_bq, L0_ffn_gate_w,
                         L0_ffn_w1, L0_ffn_w3, L0_ffn_w2, L0_ffn_e_bias, NT, E)
    in_maps = []
    for b in range(B):
        xp = np.ascontiguousarray(
            x[b].reshape(NT, 128, H).transpose(1, 0, 2)).reshape(128, NT * H)
        in_maps.append({"xp": xp, **shared})
    res = run_bass_kernel_spmd(nc, in_maps, core_ids=list(range(B)))
    outs = [r["out"].reshape(128, NT, H).transpose(1, 0, 2).reshape(NT * 128, H)
            for r in res.results]
    out = np.stack(outs)
    if out.dtype != np.float64:
        out = out.astype(np.float32)
    return out


# version 06
# speedup vs baseline: 1.1355x; 1.0014x over previous
import numpy as np
import ml_dtypes
from contextlib import ExitStack

import concourse.bass as bass
import concourse.mybir as mybir
from concourse.bass_utils import run_bass_kernel_spmd

F32 = mybir.dt.float32
BF16 = mybir.dt.bfloat16
AF = mybir.ActivationFunctionType
ALU = mybir.AluOpType
AX = mybir.AxisListType

H = 1024
NH = 8
I_DIM = 768
NI = 6
KTOP = 4
EPS = 1e-5
SCALE = 1.0 / 32.0


class Plan:
    ENGINES = ("tensor", "vector", "scalar", "sync", "gpsimd")

    def __init__(self):
        self.ops = {e: [] for e in self.ENGINES}
        self.n = {}
        self.last_wait = {e: {} for e in self.ENGINES}

    def cnt(self, sem):
        return self.n.get(sem, 0)

    def op(self, eng, fn, waits=(), inc=None, inc_n=1):
        w = {}
        for s, v in waits:
            if v is None or v <= 0:
                continue
            if self.last_wait[eng].get(s, 0) >= v:
                continue
            w[s] = max(w.get(s, 0), v)
        for s, v in w.items():
            self.last_wait[eng][s] = v
        tok = None
        if inc is not None:
            self.n[inc] = self.n.get(inc, 0) + inc_n
            tok = self.n[inc]
        self.ops[eng].append((w, fn, inc, inc_n))
        return tok

    def v(self, fn, waits=()):
        return self.op("vector", fn, waits=[("V", self.cnt("V"))] + list(waits), inc="V")

    def s(self, fn, waits=()):
        return self.op("scalar", fn, waits=[("S", self.cnt("S"))] + list(waits), inc="S")

    def t(self, fn, waits=(), inc="T"):
        return self.op("tensor", fn, waits=waits, inc=inc)


def bcast_inner(ap2d, seg, rep):
    steps = [list(p) for p in ap2d.ap[:-1]]
    return bass.AP(ap2d.tensor, ap2d.offset, steps + [[1, seg], [0, rep]])


def strided_cols(handle, off, step, cnt):
    ap = handle[:, :]
    steps = [list(ap.ap[0])] + [[step, cnt]]
    return bass.AP(ap.tensor, off, steps)


def build_nc(NT, E):
    T = NT * 128
    NQ = max(1, T // 512)
    NCH = max(1, T // 512)
    NE_COL = NT * E
    NG = (NT + 3) // 4

    nc = bass.Bass("TRN2", target_bir_lowering=False)

    NXC = min(4, NT)
    xc_tiles = NT // NXC
    xp = nc.dram_tensor("xp", [128, NT * H], F32, kind="ExternalInput").ap()
    wq = nc.dram_tensor("wq", [128, NH * H], F32, kind="ExternalInput").ap()
    bqd = nc.dram_tensor("bqd", [1, H], F32, kind="ExternalInput").ap()
    gwd = nc.dram_tensor("gwd", [128, NH * E], F32, kind="ExternalInput").ap()
    ebd = nc.dram_tensor("ebd", [128, NE_COL], F32, kind="ExternalInput").ap()
    idfd = nc.dram_tensor("idfd", [128, 128], F32, kind="ExternalInput").ap()
    idbd = nc.dram_tensor("idbd", [128, 128], BF16, kind="ExternalInput").ap()
    w1d = nc.dram_tensor("w1d", [E, 128, NH * I_DIM], F32, kind="ExternalInput").ap()
    w3d = nc.dram_tensor("w3d", [E, 128, NH * I_DIM], F32, kind="ExternalInput").ap()
    w2d = nc.dram_tensor("w2d", [E, 128, NI * H], F32, kind="ExternalInput").ap()
    outd = nc.dram_tensor("out", [128, NT * H], F32, kind="ExternalOutput").ap()

    A_LN1T = 0
    A_WQ = NT * 1024
    A_SIZE = max(A_WQ + NH * H, NT * T, 32768)
    WB = NH * I_DIM
    B0_W1, B0_W3, B0_W2 = 0, WB, 2 * WB
    B1_W1, B1_W3 = 3 * WB, 4 * WB
    B1_W2A = 5 * WB
    B1_W2A_LEN = A_SIZE - B1_W2A
    assert A_SIZE >= B1_W2A
    B1_W2B_LEN = WB - B1_W2A_LEN

    B_AT = 4096
    B_S = B_AT + 2 * NI * 512
    B_SIZE = max(NT * 1024, B_S + 2 * NI * 512)
    B_MACC = B_AT
    B_MSK = B_AT + NE_COL

    ST_BN = 0
    ST_MV = 24
    ST_RSTD = 24 + 2 * NT
    ST_NEGMUR = ST_RSTD + NT
    ST_ALPHA = ST_NEGMUR + NT
    ST_Z4 = ST_ALPHA + NT
    ST_ZSUM = ST_Z4 + 2 * NQ
    ST_RMAX = ST_ZSUM + 2
    ST_DEN = ST_RMAX + NT
    ST_RCP = ST_DEN + NT
    ST_TMP = ST_RCP + NT
    ST_SIZE = ST_TMP + 12
    assert ST_SIZE <= 192

    st = ExitStack()
    with st:
        T_x = st.enter_context(nc.sbuf_tensor("T_x", [128, NT * H], F32))
        T_fm = st.enter_context(nc.sbuf_tensor("T_fm", [128, NT * 1024], BF16))
        T_A = st.enter_context(nc.sbuf_tensor("T_A", [128, A_SIZE], BF16))
        T_B = st.enter_context(nc.sbuf_tensor("T_B", [128, B_SIZE], BF16))
        T_st = st.enter_context(nc.sbuf_tensor("T_st", [128, 192], F32))
        T_c = st.enter_context(nc.sbuf_tensor("T_c", [128, NE_COL], F32))
        T_lg = st.enter_context(nc.sbuf_tensor("T_lg", [128, NE_COL], F32))
        T_eb = st.enter_context(nc.sbuf_tensor("T_eb", [128, NE_COL], F32))
        T_gw = st.enter_context(nc.sbuf_tensor("T_gw", [128, NH * E], F32))
        T_x2t = st.enter_context(nc.sbuf_tensor("T_x2t", [128, 1024], F32))
        T_srep = st.enter_context(nc.sbuf_tensor("T_srep", [128, E], F32))
        T_ones = st.enter_context(nc.sbuf_tensor("T_ones", [128, 128], F32))
        bq_bf = st.enter_context(nc.sbuf_tensor("bq_bf", [1, H], BF16))
        ones_b = st.enter_context(nc.sbuf_tensor("ones_b", [1, 128], BF16))
        id_b = st.enter_context(nc.sbuf_tensor("id_b", [128, 128], BF16))
        id_f = st.enter_context(nc.sbuf_tensor("id_f", [128, 128], F32))

        P0 = st.enter_context(nc.psum_tensor("P0", [128, 2048], F32))
        P1 = st.enter_context(nc.psum_tensor("P1", [128, 1024], F32))
        PTB = st.enter_context(nc.psum_tensor("PTB", [128, 1024], BF16))
        PTB2 = st.enter_context(nc.psum_tensor("PTB2", [128, 1024], BF16))

        sems = {}
        sem_names = ["dW", "dA1", "dA3", "dA2", "dB1", "dB3", "dB2",
                     "dO", "dI", "dG", "T", "S", "V"] + \
            [f"dX{i}" for i in range(NXC)]
        for name in sem_names:
            sems[name] = st.enter_context(nc.semaphore(name))

        p = Plan()

        def d(fn, eng="sync", waits=(), inc="dX0"):
            return p.op(eng, fn, waits=waits, inc=inc, inc_n=16)

        d(lambda e: e.dma_start(out=T_x[:, 0:xc_tiles * H],
                                in_=xp[:, 0:xc_tiles * H]), inc="dX0")
        d(lambda e: e.dma_start(out=id_b[:, :], in_=idbd), inc="dI")
        d(lambda e: e.dma_start(out=id_f[:, :], in_=idfd), inc="dI")
        for ci in range(1, NXC):
            d(lambda e, ci=ci: e.dma_start(
                out=T_x[:, ci * xc_tiles * H:(ci + 1) * xc_tiles * H],
                in_=xp[:, ci * xc_tiles * H:(ci + 1) * xc_tiles * H]),
                inc=f"dX{ci}")
        d(lambda e: e.dma_start(out=T_gw[:, :], in_=gwd), inc="dG")
        d(lambda e: e.dma_start(out=T_eb[:, :], in_=ebd), inc="dG")
        dG_all = p.cnt("dG")

        d(lambda e: e.dma_start(out=T_A[:, A_WQ:A_WQ + NH * H], in_=wq),
          eng="gpsimd", inc="dW")
        d(lambda e: e.dma_start(out=bq_bf[:, :], in_=bqd), eng="gpsimd", inc="dW")
        dW_all = p.cnt("dW")

        v_ones_b = p.v(lambda e: e.memset(ones_b[:, :], 1.0))
        v_ones_m = p.v(lambda e: e.memset(T_ones[:, :], 1.0))

        def col(c0, n=1):
            return T_st[:, c0:c0 + n]

        def emit_ln_stats_group(g, x_of, xwaits_of):
            tiles = list(range(4 * g, min(4 * g + 4, NT)))
            for j in tiles:
                sl = ST_BN + (j % 2) * 12
                x_ap = x_of(j)
                p.v(lambda e, a=x_ap, sl=sl: e.bn_stats(col(sl, 6), a[:, 0:512]),
                    waits=xwaits_of(j))
                p.v(lambda e, a=x_ap, sl=sl: e.bn_stats(col(sl + 6, 6), a[:, 512:1024]))
                p.v(lambda e, j=j, sl=sl: e.bn_aggr(col(ST_MV + 2 * j, 2), col(sl, 12)))
            nt_g = len(tiles)
            j0 = tiles[0]
            var_s = strided_cols(T_st, ST_MV + 2 * j0 + 1, 2, nt_g)
            mean_s = strided_cols(T_st, ST_MV + 2 * j0, 2, nt_g)
            rstd = col(ST_RSTD + j0, nt_g)
            negmur = col(ST_NEGMUR + j0, nt_g)
            t_y2 = col(ST_TMP, nt_g)
            t_y2v = col(ST_TMP + 4, nt_g)
            t_f = col(ST_TMP + 8, nt_g)
            vv = p.v(lambda e: e.tensor_scalar_add(var_s, var_s, EPS))
            ts = p.s(lambda e: e.activation(t_y2, var_s, AF.Sqrt), waits=[("V", vv)])
            p.v(lambda e: e.reciprocal(rstd, t_y2), waits=[("S", ts)])
            p.v(lambda e: e.tensor_tensor(t_y2, rstd, rstd, ALU.mult))
            p.v(lambda e: e.tensor_tensor(t_y2v, t_y2, var_s, ALU.mult))
            p.v(lambda e: e.tensor_scalar(t_f, t_y2v, -0.5, 1.5, ALU.mult, ALU.add))
            p.v(lambda e: e.tensor_tensor(rstd, rstd, t_f, ALU.mult))
            p.v(lambda e: e.scalar_tensor_tensor(negmur, mean_s, -1.0, rstd,
                                                 ALU.mult, ALU.mult))

        def emit_ln_apply(j, x_ap, lnf_ap, extra=()):
            return p.v(lambda e, j=j, o=lnf_ap, a=x_ap: e.tensor_scalar(
                o, a, col(ST_RSTD + j), col(ST_NEGMUR + j), ALU.mult, ALU.add),
                waits=list(extra))

        def emit_ln_group(g, x_of, lnf_of, xwaits_of):
            emit_ln_stats_group(g, x_of, xwaits_of)
            return [emit_ln_apply(j, x_of(j), lnf_of(j))
                    for j in range(4 * g, min(4 * g + 4, NT))]

        def fmv(handle, j):
            return handle[:, 0:NT * 1024].rearrange(
                "p (c t) -> p c t", c=NH)[:, :, j * 128:(j + 1) * 128]

        s_ev_ln1 = [None] * NT
        t_q = [None] * NT
        s_ev_qtok = [None] * NT
        t_tr2 = [None] * NT
        s_ev_qt = [None] * NT
        t_tr1 = [None] * NT
        lnf_toks = [None] * NT

        for g in range(NG):
            toks = emit_ln_group(
                g,
                x_of=lambda j: T_x[:, j * H:(j + 1) * H],
                lnf_of=lambda j: T_B[:, j * 1024:(j + 1) * 1024],
                xwaits_of=lambda j: [(f"dX{j // xc_tiles}", 16)])
            for i, j in enumerate(range(4 * g, min(4 * g + 4, NT))):
                lnf_toks[j] = toks[i]

        def p1_tr_ln(j):
            def tr_ln(e, j=j):
                for k in range(NH):
                    ins = e.transpose(
                        PTB[:, k * 128:(k + 1) * 128],
                        T_B[:, j * 1024 + k * 128: j * 1024 + k * 128 + 128],
                        id_b[:, :])
                return ins
            t_tr1[j] = p.t(tr_ln, waits=[("V", lnf_toks[j]), ("dI", 32),
                                         ("S", s_ev_ln1[j - 1] if j else 0)])
            s_ev_ln1[j] = p.s(lambda e, j=j: e.copy(
                fmv(T_A, j),
                PTB[:, :].rearrange("p (c t) -> p c t", c=NH)),
                waits=[("T", t_tr1[j])])

        def p1_q(j):
            def qmm(e, j=j):
                base = (j % 2) * 1024
                for half in range(2):
                    for hc in range(NH):
                        e.matmul(P0[:, base + half * 512: base + half * 512 + 512],
                                 T_A[:, A_LN1T + hc * (NT * 128) + j * 128:
                                     A_LN1T + hc * (NT * 128) + j * 128 + 128],
                                 T_A[:, A_WQ + hc * H + half * 512:
                                     A_WQ + hc * H + half * 512 + 512],
                                 start=(hc == 0), stop=False)
                    ins = e.matmul(P0[:, base + half * 512: base + half * 512 + 512],
                                   ones_b[:, :],
                                   bq_bf[:, half * 512: half * 512 + 512],
                                   start=False, stop=True)
                return ins
            t_q[j] = p.t(qmm, waits=[("S", s_ev_ln1[j]), ("dW", dW_all),
                                     ("V", v_ones_b),
                                     ("S", s_ev_qtok[j - 2] if j >= 2 else 0)])
            s_ev_qtok[j] = p.s(lambda e, j=j: e.copy(
                T_B[:, j * 1024:(j + 1) * 1024],
                P0[:, (j % 2) * 1024:(j % 2) * 1024 + 1024]),
                waits=[("T", t_q[j])])

        def p1_tr_q(j):
            def tr_q(e, j=j):
                for k in range(NH):
                    ins = e.transpose(
                        PTB2[:, k * 128:(k + 1) * 128],
                        T_B[:, j * 1024 + k * 128: j * 1024 + k * 128 + 128],
                        id_b[:, :])
                return ins
            t_tr2[j] = p.t(tr_q, waits=[("S", s_ev_qtok[j]),
                                        ("S", s_ev_qt[j - 1] if j else 0)])
            s_ev_qt[j] = p.s(lambda e, j=j: e.copy(
                fmv(T_fm, j),
                PTB2[:, :].rearrange("p (c t) -> p c t", c=NH)),
                waits=[("T", t_tr2[j])])

        LEAD = 6
        for k in range(NT + LEAD + 1):
            if k < NT:
                p1_tr_ln(k)
            if 0 <= k - LEAD < NT:
                p1_q(k - LEAD)
            if 0 <= k - LEAD - 1 < NT:
                p1_tr_q(k - LEAD - 1)

        t_sq = [[None] * NQ for _ in range(NT)]
        s_exp = [[None] * NQ for _ in range(NT)]
        v_alpha = [None] * NT
        for j in range(NT):
            for q in range(NQ):
                def smm(e, j=j, q=q):
                    for dc in range(NH):
                        ins = e.matmul(
                            P0[:, q * 512:(q + 1) * 512],
                            T_fm[:, dc * (NT * 128) + j * 128:
                                 dc * (NT * 128) + j * 128 + 128],
                            T_fm[:, dc * (NT * 128) + q * 512:
                                 dc * (NT * 128) + q * 512 + 512],
                            start=(dc == 0), stop=(dc == NH - 1))
                    return ins
                t_sq[j][q] = p.t(smm, waits=[
                    ("S", s_ev_qt[NT - 1]),
                    ("S", s_exp[j - 1][q] if j else 0)])
                s_exp[j][q] = p.s(lambda e, j=j, q=q: e.activation(
                    T_A[:, j * T + q * 512: j * T + q * 512 + 512],
                    P0[:, q * 512:(q + 1) * 512], AF.Exp, scale=SCALE),
                    waits=[("T", t_sq[j][q])])
            zsum = col(ST_ZSUM + (j % 2))
            vz = p.v(lambda e, j=j, o=zsum: e.reduce_sum(
                o, T_A[:, j * T: j * T + T], axis=AX.X),
                waits=[("S", s_exp[j][NQ - 1])])
            v_alpha[j] = p.v(lambda e, j=j, zs=zsum: e.reciprocal(
                col(ST_ALPHA + j), zs))

        t_ao = [[None] * 2 for _ in range(NT)]
        v_ev3 = [[None] * 2 for _ in range(NT)]
        s_exp_all = p.cnt("S")
        for j in range(NT):
            for hh in range(2):
                slot = (2 * j + hh) % 4
                def aomm(e, j=j, hh=hh, slot=slot):
                    for i in range(NT):
                        ins = e.matmul(
                            P0[:, slot * 512: slot * 512 + 512],
                            T_A[:, i * T + j * 128: i * T + j * 128 + 128],
                            T_B[:, i * 1024 + hh * 512: i * 1024 + hh * 512 + 512],
                            start=(i == 0), stop=(i == NT - 1))
                    return ins
                prev = v_ev3[(2 * j + hh - 4) // 2][(2 * j + hh - 4) % 2] \
                    if 2 * j + hh >= 4 else 0
                t_ao[j][hh] = p.t(aomm, waits=[("S", s_exp_all), ("V", prev)])
                v_ev3[j][hh] = p.v(lambda e, j=j, hh=hh, slot=slot: e.scalar_tensor_tensor(
                    T_x[:, j * H + hh * 512: j * H + hh * 512 + 512],
                    P0[:, slot * 512: slot * 512 + 512],
                    col(ST_ALPHA + j),
                    T_x[:, j * H + hh * 512: j * H + hh * 512 + 512],
                    ALU.mult, ALU.add),
                    waits=[("T", t_ao[j][hh]), ("V", v_alpha[j])])
            if j % 4 == 3:
                emit_ln_stats_group(
                    j // 4,
                    x_of=lambda jj: T_x[:, jj * H:(jj + 1) * H],
                    xwaits_of=lambda jj: [("V", v_ev3[jj][1])])

        t_p3_all = p.cnt("T")
        v_p3_all = p.cnt("V")

        def srep_mm(e):
            for hc in range(NH):
                ins = e.matmul(P0[:, 512:512 + E], T_ones[:, :],
                               T_gw[:, hc * E:(hc + 1) * E],
                               start=(hc == 0), stop=(hc == NH - 1))
            return ins
        t_srep = p.t(srep_mm, waits=[("V", v_ones_m), ("dG", dG_all),
                                     ("V", v_p3_all)])
        v_srep = p.v(lambda e: e.tensor_copy(T_srep[:, :], P0[:, 512:512 + E]),
                     waits=[("T", t_srep)])

        s_ev_ln2 = [None] * NT
        s_x2t = [None] * NT
        t_x2tr = [None] * NT
        t_tr4 = [None] * NT
        t_rt = [None] * NT
        s_lgev = [None] * NT
        v_fix = [None] * NT
        lnf2_toks = [None] * NT

        def emit_fixups(tiles):
            for j in tiles:
                p.v(lambda e, j=j: e.tensor_scalar_mul(
                    T_lg[:, j * E:(j + 1) * E], T_lg[:, j * E:(j + 1) * E],
                    col(ST_RSTD + j)),
                    waits=[("S", s_lgev[j])])
                v_fix[j] = p.v(lambda e, j=j: e.scalar_tensor_tensor(
                    T_lg[:, j * E:(j + 1) * E], T_srep[:, :],
                    col(ST_NEGMUR + j), T_lg[:, j * E:(j + 1) * E],
                    ALU.mult, ALU.add))

        for j in range(NT):
            lnf2_toks[j] = emit_ln_apply(
                j, T_x[:, j * H:(j + 1) * H], T_B[:, j * 1024:(j + 1) * 1024],
                extra=([("T", t_p3_all), ("V", v_srep)] if j == 0 else []))

        def p4_x2tr(j):
            def x2tr(e, j=j):
                for k in range(NH):
                    ins = e.transpose(
                        P1[:, k * 128:(k + 1) * 128],
                        T_x[:, j * H + k * 128: j * H + k * 128 + 128],
                        id_f[:, :])
                return ins
            t_x2tr[j] = p.t(x2tr, waits=[("V", v_ev3[j][1]),
                                         ("S", s_x2t[j - 1] if j else 0)])

        def p4_lnrt(j):
            def tr_ln2(e, j=j):
                for k in range(NH):
                    ins = e.transpose(
                        PTB[:, k * 128:(k + 1) * 128],
                        T_B[:, j * 1024 + k * 128: j * 1024 + k * 128 + 128],
                        id_b[:, :])
                return ins
            t_tr4[j] = p.t(tr_ln2, waits=[("V", lnf2_toks[j]),
                                          ("S", s_ev_ln2[j - 1] if j else 0)])

            def rtmm(e, j=j):
                for hc in range(NH):
                    ins = e.matmul(P0[:, 0:E],
                                   T_x2t[:, hc * 128:(hc + 1) * 128],
                                   T_gw[:, hc * E:(hc + 1) * E],
                                   start=(hc == 0), stop=(hc == NH - 1))
                return ins
            t_rt[j] = p.t(rtmm, waits=[("S", s_x2t[j]), ("dG", dG_all),
                                       ("S", s_lgev[j - 1] if j else 0)])

        for k in range(NT + 1):
            if k < NT:
                p4_x2tr(k)
            if 0 <= k - 1 < NT:
                p4_lnrt(k - 1)
            if k < NT:
                s_x2t[k] = p.s(lambda e, k=k: e.copy(T_x2t[:, :], P1[:, :]),
                               waits=[("T", t_x2tr[k]),
                                      ("T", t_rt[k - 1] if k else 0)])
            if 0 <= k - 1 < NT:
                jj = k - 1
                s_ev_ln2[jj] = p.s(lambda e, j=jj: e.copy(
                    fmv(T_fm, j),
                    PTB[:, :].rearrange("p (c t) -> p c t", c=NH)),
                    waits=[("T", t_tr4[jj])])
                s_lgev[jj] = p.s(lambda e, j=jj: e.copy(
                    T_lg[:, j * E:(j + 1) * E], P0[:, 0:E]),
                    waits=[("T", t_rt[jj])])
        emit_fixups(range(NT))

        macc = T_B[:, B_MACC:B_MACC + NE_COL]
        msk = T_B[:, B_MSK:B_MSK + NE_COL]
        rmax = col(ST_RMAX, NT)
        den = col(ST_DEN, NT)
        rcpd = col(ST_RCP, NT)

        s_sig = p.s(lambda e: e.activation(T_lg[:, :], T_lg[:, :], AF.Sigmoid),
                    waits=[("V", v_fix[NT - 1])])
        d3 = lambda ap: ap.rearrange("p (s e) -> p s e", s=NT)
        p.v(lambda e: e.tensor_add(T_eb[:, :], T_lg[:, :], T_eb[:, :]),
            waits=[("S", s_sig)])
        for k in range(KTOP):
            p.v(lambda e: e.reduce_max(
                rmax.rearrange("p (s o) -> p s o", s=NT), d3(T_eb[:, :]), axis=AX.X))
            p.v(lambda e: e.tensor_tensor(
                d3(msk), d3(T_eb[:, :]), bcast_inner(rmax, NT, E), ALU.is_ge))
            if k == 0:
                p.v(lambda e: e.tensor_copy(macc, msk))
            else:
                p.v(lambda e: e.tensor_add(macc, macc, msk))
            if k < KTOP - 1:
                p.v(lambda e: e.scalar_tensor_tensor(
                    T_eb[:, :], msk, -1e30, T_eb[:, :], ALU.mult, ALU.add))
        p.v(lambda e: e.tensor_tensor(T_lg[:, :], T_lg[:, :], macc, ALU.mult))
        p.v(lambda e: e.reduce_sum(
            den.rearrange("p (s o) -> p s o", s=NT), d3(T_lg[:, :]), axis=AX.X))
        p.v(lambda e: e.reciprocal(rcpd, den))
        v_c = p.v(lambda e: e.tensor_tensor(
            d3(T_c[:, :]), d3(T_lg[:, :]), bcast_inner(rcpd, NT, E), ALU.mult))

        s_ln2_all = s_lgev[NT - 1]

        def w1_ap(par, hc, ic):
            off = (B0_W1 if par == 0 else B1_W1) + hc * I_DIM + ic * 128
            return T_A[:, off: off + 128]

        def w3_ap(par, hc, ic):
            off = (B0_W3 if par == 0 else B1_W3) + hc * I_DIM + ic * 128
            return T_A[:, off: off + 128]

        def w2_ap(par, ic, hh):
            off = ic * H + hh * 512
            if par == 0:
                return T_A[:, B0_W2 + off: B0_W2 + off + 512]
            if off + 512 <= B1_W2A_LEN:
                return T_A[:, B1_W2A + off: B1_W2A + off + 512]
            off -= B1_W2A_LEN
            return T_B[:, off: off + 512]

        def at_ap(par, ic, sl=None):
            off = B_AT + par * (NI * 512) + ic * 512
            if sl is None:
                return T_B[:, off: off + 512]
            return T_B[:, off + sl[0]: off + sl[1]]

        def s_ap(par, ic):
            off = B_S + par * (NI * 512) + ic * 512
            return T_B[:, off: off + 512]

        exp_dma_tok = [None] * E
        t_out_done = [None] * E

        def issue_expert(e_idx, gate_toks):
            par = e_idx % 2
            s1, s3, s2 = ("dA1", "dA3", "dA2") if par == 0 else ("dB1", "dB3", "dB2")
            t1 = p.op("gpsimd", lambda e, i=e_idx, par=par: e.dma_start(
                out=T_A[:, (B0_W1 if par == 0 else B1_W1):
                        (B0_W1 if par == 0 else B1_W1) + WB],
                in_=w1d[i, :, :]), waits=gate_toks, inc=s1, inc_n=16)
            t3 = p.op("gpsimd", lambda e, i=e_idx, par=par: e.dma_start(
                out=T_A[:, (B0_W3 if par == 0 else B1_W3):
                        (B0_W3 if par == 0 else B1_W3) + WB],
                in_=w3d[i, :, :]), inc=s3, inc_n=16)
            if par == 0:
                t2 = p.op("gpsimd", lambda e, i=e_idx: e.dma_start(
                    out=T_A[:, B0_W2: B0_W2 + WB], in_=w2d[i, :, :]),
                    inc=s2, inc_n=16)
            else:
                p.op("gpsimd", lambda e, i=e_idx: e.dma_start(
                    out=T_A[:, B1_W2A: B1_W2A + B1_W2A_LEN],
                    in_=w2d[i, :, 0:B1_W2A_LEN]), inc=s2, inc_n=16)
                t2 = p.op("gpsimd", lambda e, i=e_idx: e.dma_start(
                    out=T_B[:, 0:B1_W2B_LEN],
                    in_=w2d[i, :, B1_W2A_LEN:WB]), inc=s2, inc_n=16)
            exp_dma_tok[e_idx] = {"w1": (s1, t1), "w3": (s3, t3), "w2": (s2, t2)}

        issue_expert(0, [("T", t_p3_all)])
        if E > 1:
            issue_expert(1, [("T", t_tr4[min(3, NT - 1)])])

        NGC = E * NCH
        t_g = {}
        t_u = {}
        s_silu = {}
        v_a = {}
        t_og = {}
        v_oev = {}
        out_slots = [P1[:, 0:512], P1[:, 512:1024]]
        oslot_war = {0: ("S", s_x2t[NT - 1]), 1: ("S", s_x2t[NT - 1])}
        bank_war = {0: ("S", s_ln2_all), 1: ("V", v_srep),
                    2: ("V", v_p3_all), 3: ("V", v_p3_all)}

        def emit_chunk(gc):
            e_i, c = divmod(gc, NCH)
            par = e_i % 2
            gpar = gc % 2
            toks = exp_dma_tok[e_i]
            for ic in range(NI):
                gb, ub = ic % 2, 2 + ic % 2

                def gmm(e, c=c, par=par, ic=ic, gb=gb):
                    for hc in range(NH):
                        ins = e.matmul(
                            P0[:, gb * 512: gb * 512 + 512],
                            w1_ap(par, hc, ic),
                            T_fm[:, hc * (NT * 128) + c * 512:
                                 hc * (NT * 128) + c * 512 + 512],
                            start=(hc == 0), stop=(hc == NH - 1))
                    return ins
                t_g[(gc, ic)] = p.t(gmm, waits=[toks["w1"], ("S", s_ln2_all),
                                                bank_war[gb]])

                def umm(e, c=c, par=par, ic=ic, ub=ub):
                    for hc in range(NH):
                        ins = e.matmul(
                            P0[:, ub * 512: ub * 512 + 512],
                            w3_ap(par, hc, ic),
                            T_fm[:, hc * (NT * 128) + c * 512:
                                 hc * (NT * 128) + c * 512 + 512],
                            start=(hc == 0), stop=(hc == NH - 1))
                    return ins
                t_u[(gc, ic)] = p.t(umm, waits=[toks["w3"], bank_war[ub]])

                w = [("T", t_g[(gc, ic)])]
                if gc >= 2:
                    w.append(("V", v_a[(gc - 2, ic)]))
                s_silu[(gc, ic)] = p.s(lambda e, gpar=gpar, ic=ic, gb=gb: e.activation(
                    s_ap(gpar, ic), P0[:, gb * 512: gb * 512 + 512], AF.Silu),
                    waits=w)
                bank_war[gb] = ("S", s_silu[(gc, ic)])
                w = [("T", t_u[(gc, ic)]), ("S", s_silu[(gc, ic)])]
                if gc >= 2:
                    w.append(("T", t_og[(gc - 2, 7)]))
                if gc < 2:
                    w.append(("V", v_c))
                v_a[(gc, ic)] = p.v(lambda e, gpar=gpar, ic=ic, ub=ub: e.tensor_tensor(
                    at_ap(gpar, ic),
                    s_ap(gpar, ic),
                    P0[:, ub * 512: ub * 512 + 512], ALU.mult),
                    waits=w)
                bank_war[ub] = ("V", v_a[(gc, ic)])

        def emit_out(gc):
            e_i, c = divmod(gc, NCH)
            par = e_i % 2
            gpar = gc % 2
            toks = exp_dma_tok[e_i]
            grp = 0
            for ts in range(4):
                for hh in range(2):
                    slot = grp % 2
                    def omm(e, par=par, gpar=gpar, ts=ts, hh=hh, slot=slot):
                        for ic in range(NI):
                            ins = e.matmul(
                                out_slots[slot],
                                at_ap(gpar, ic, (ts * 128, ts * 128 + 128)),
                                w2_ap(par, ic, hh),
                                start=(ic == 0), stop=(ic == NI - 1))
                        return ins
                    w = [("V", v_a[(gc, NI - 1)]), toks["w2"], oslot_war[slot]]
                    t_og[(gc, grp)] = p.t(omm, waits=w)
                    tile = c * 4 + ts
                    v_oev[(gc, grp)] = p.v(
                        lambda e, e_i=e_i, ts=ts, hh=hh, slot=slot, tile=tile:
                        e.scalar_tensor_tensor(
                            T_x[:, tile * H + hh * 512: tile * H + hh * 512 + 512],
                            out_slots[slot],
                            T_c[:, tile * E + e_i: tile * E + e_i + 1],
                            T_x[:, tile * H + hh * 512: tile * H + hh * 512 + 512],
                            ALU.mult, ALU.add),
                        waits=[("T", t_og[(gc, grp)]), ("V", v_c)])
                    oslot_war[slot] = ("V", v_oev[(gc, grp)])
                    grp += 1
            if c == NCH - 1:
                t_out_done[e_i] = p.cnt("T")

        def after_out(x):
            e_done, c_done = divmod(x, NCH)
            if c_done == NCH - 1 and e_done + 2 < E:
                issue_expert(e_done + 2, [("T", t_out_done[e_done])])

        emit_chunk(0)
        if NGC > 1:
            emit_chunk(1)
        for gc in range(2, NGC):
            emit_out(gc - 2)
            after_out(gc - 2)
            emit_chunk(gc)
        if NGC > 1:
            emit_out(NGC - 2)
            after_out(NGC - 2)
        emit_out(NGC - 1)

        n_store = 0
        for c in range(NCH):
            gc_last = (E - 1) * NCH + c
            vtok = v_oev[(gc_last, 7)]
            p.op("sync", lambda e, c=c: e.dma_start(
                out=outd[:, c * 4 * H: (c + 1) * 4 * H],
                in_=T_x[:, c * 4 * H: (c + 1) * 4 * H]),
                waits=[("V", vtok)], inc="dO", inc_n=16)
            n_store += 1
        p.op("sync", lambda e, n=n_store: e.wait_ge(sems["dO"], 16 * n))

        with nc.Block() as block:
            def runner(eng_name):
                def body(eng):
                    for w, fn, inc, inc_n in p.ops[eng_name]:
                        for s_name, v in sorted(w.items()):
                            eng.wait_ge(sems[s_name], v)
                        ins = fn(eng)
                        if inc is not None and ins is not None:
                            ins.then_inc(sems[inc], inc_n)
                return body
            block.tensor(runner("tensor"))
            block.vector(runner("vector"))
            block.scalar(runner("scalar"))
            block.sync(runner("sync"))
            block.gpsimd(runner("gpsimd"))

    return nc


_NC_CACHE = {}


def get_nc(NT, E):
    key = (NT, E)
    if key not in _NC_CACHE:
        _NC_CACHE[key] = build_nc(NT, E)
    return _NC_CACHE[key]


def make_shared(Wq, bq, gate_w, w1, w3, w2, e_bias, NT, E):
    f = np.float32
    wq_p = np.ascontiguousarray(
        np.asarray(Wq, f).reshape(NH, 128, H).transpose(1, 0, 2)).reshape(128, NH * H)
    bq_p = np.asarray(bq, f).reshape(1, H)
    gw_p = np.ascontiguousarray(
        np.asarray(gate_w, f).T.reshape(NH, 128, E).transpose(1, 0, 2)).reshape(128, NH * E)
    eb_p = np.ascontiguousarray(np.broadcast_to(
        np.tile(np.asarray(e_bias, f).reshape(1, E), (1, NT)), (128, NT * E)))
    w1_p = np.ascontiguousarray(
        np.asarray(w1, f).reshape(E, NH, 128, I_DIM).transpose(0, 2, 1, 3)).reshape(E, 128, NH * I_DIM)
    w3_p = np.ascontiguousarray(
        np.asarray(w3, f).reshape(E, NH, 128, I_DIM).transpose(0, 2, 1, 3)).reshape(E, 128, NH * I_DIM)
    w2_p = np.ascontiguousarray(
        np.asarray(w2, f).reshape(E, NI, 128, H).transpose(0, 2, 1, 3)).reshape(E, 128, NI * H)
    id_f = np.eye(128, dtype=f)
    id_b = np.eye(128, dtype=ml_dtypes.bfloat16)
    return {"wq": wq_p, "bqd": bq_p, "gwd": gw_p, "ebd": eb_p,
            "idfd": id_f, "idbd": id_b, "w1d": w1_p, "w3d": w3_p, "w2d": w2_p}


def kernel(L0_attn_inputs, L0_attn_Wq, L0_attn_bq, L0_ffn_gate_w,
           L0_ffn_w1, L0_ffn_w3, L0_ffn_w2, L0_ffn_e_bias):
    NT, E = 16, 32
    x = np.asarray(L0_attn_inputs, np.float32)
    B = x.shape[0]
    nc = get_nc(NT, E)
    shared = make_shared(L0_attn_Wq, L0_attn_bq, L0_ffn_gate_w,
                         L0_ffn_w1, L0_ffn_w3, L0_ffn_w2, L0_ffn_e_bias, NT, E)
    in_maps = []
    for b in range(B):
        xp = np.ascontiguousarray(
            x[b].reshape(NT, 128, H).transpose(1, 0, 2)).reshape(128, NT * H)
        in_maps.append({"xp": xp, **shared})
    res = run_bass_kernel_spmd(nc, in_maps, core_ids=list(range(B)))
    outs = [r["out"].reshape(128, NT, H).transpose(1, 0, 2).reshape(NT * 128, H)
            for r in res.results]
    out = np.stack(outs)
    if out.dtype != np.float64:
        out = out.astype(np.float32)
    return out
